# version 30
# baseline (speedup 1.0000x reference)
from contextlib import ExitStack

import numpy as np
import ml_dtypes
import concourse.bass as bass
import concourse.mybir as mybir
from concourse.bass_utils import run_bass_kernel_spmd

F32 = mybir.dt.float32
BF16 = mybir.dt.bfloat16
I32 = mybir.dt.int32
AF = mybir.ActivationFunctionType
ALU = mybir.AluOpType
AX = mybir.AxisListType

MAGIC = 0x5F3759DF
YIELDS = True
DBG_SLOT0 = False
YSET = {"tp", "kv", "latq", "proj", "h"}
SIG = 1.0 / 127.0
EPS = 1e-5


class Tok:
    __slots__ = ("sem", "val")

    def __init__(self, sem, val):
        self.sem = sem
        self.val = val


class Prog:
    def __init__(self, nc, es):
        self.nc = nc
        self.es = es
        self.items = []
        self._sems = {}
        self._producers = {}

    def _sem(self, name):
        if name not in self._sems:
            h = self.es.enter_context(self.nc.semaphore(name))
            self._sems[name] = [h, 0]
        return self._sems[name]

    def op(self, engine, emit, waits=(), out=None, dma=False):
        tok = None
        inc_sem = None
        inc_by = 0
        if out is not None:
            ent = self._sem(out)
            inc_by = 16 if dma else 1
            ent[1] += inc_by
            tok = Tok(ent[0], ent[1])
            inc_sem = ent[0]
            self._producers.setdefault(id(ent[0]), set()).add(engine)
        ws = tuple(t for t in waits if t is not None)
        self.items.append((engine, emit, ws, inc_sem, inc_by))
        return tok

    def barrier_tok(self, name):
        ent = self._sem(name)
        return Tok(ent[0], ent[1])

    def emit(self, block):
        streams = {}
        for it in self.items:
            streams.setdefault(it[0], []).append(it)

        def mk_body(ops, ename):
            def body(eng):
                seen = {}
                for _, emit, waits, inc_sem, inc_by in ops:
                    per = {}
                    for t in waits:
                        k = id(t.sem)
                        if k not in per or per[k][1] < t.val:
                            per[k] = (t.sem, t.val)
                    for k, (sem, val) in per.items():
                        if seen.get(k, -1) >= val:
                            continue
                        if ename == "tensor" and self._producers.get(k) == {"tensor"}:
                            seen[k] = max(seen.get(k, -1), val)
                            continue
                        eng.wait_ge(sem, val)
                        seen[k] = val
                    ins = emit(eng)
                    if inc_sem is not None:
                        ins.then_inc(inc_sem, inc_by)
            return body

        for name, ops in streams.items():
            getattr(block, name)(mk_body(ops, name))


class Cfg:
    def __init__(self, T=8192, D=1024, NH=16, HD=64, LAT=512, FH=3072):
        self.T, self.D, self.NH, self.HD, self.LAT, self.FH = T, D, NH, HD, LAT, FH
        self.J = T // 128
        assert self.J % 4 == 0
        self.S = self.J // 4
        assert self.S % 4 == 0, "FFN blocks cover 4 slots"
        self.DK = D // 128
        self.LK = LAT // 128
        self.FM = FH // 128
        self.QC = NH * HD // 128
        self.NHG = NH // 2
        self.OWN = self.S * 128
        self.QS_W = NH * 128
        self.HGW = self.NHG * 128


FULL = Cfg()


def build_program(cfg: Cfg):
    nc = bass.Bass("TRN2", target_bir_lowering=False)
    T, D, NH, HD, LAT, FH = cfg.T, cfg.D, cfg.NH, cfg.HD, cfg.LAT, cfg.FH
    J, S, DK, LK, FM, QC = cfg.J, cfg.S, cfg.DK, cfg.LK, cfg.FM, cfg.QC
    NHG, OWN, QS_W, HGW = cfg.NHG, cfg.OWN, cfg.QS_W, cfg.HGW
    NB2 = J // 2

    xb_d = nc.dram_tensor("xb", [T, D], F32, kind="ExternalInput")
    xo_d = nc.dram_tensor("xo", [OWN, D], F32, kind="ExternalInput")
    wc_d = nc.dram_tensor("wc", [D, LAT], F32, kind="ExternalInput")
    wq_d = nc.dram_tensor("wq", [LAT, D], F32, kind="ExternalInput")
    wct_d = nc.dram_tensor("wct", [LAT, D], F32, kind="ExternalInput")
    wkv_d = nc.dram_tensor("wkv", [LAT, 2 * HD], F32, kind="ExternalInput")
    wproj_d = nc.dram_tensor("wproj", [D, D], F32, kind="ExternalInput")
    wg_d = nc.dram_tensor("wg", [DK, 128, FH], I32, kind="ExternalInput")
    wu_d = nc.dram_tensor("wu", [DK, 128, FH], I32, kind="ExternalInput")
    wd_d = nc.dram_tensor("wd", [FH, D], I32, kind="ExternalInput")
    ident_d = nc.dram_tensor("ident", [128, 128], BF16, kind="ExternalInput")
    ident2_d = nc.dram_tensor("ident2", [128, HD], BF16, kind="ExternalInput")
    masks_d = nc.dram_tensor("masks", [4, 128, 128], BF16, kind="ExternalInput")
    sel_d = nc.dram_tensor("sel", [128, 4], F32, kind="ExternalInput")
    out_d = nc.dram_tensor("out", [OWN, D], F32, kind="ExternalOutput")
    wgbf_d = nc.dram_tensor("wgbf", [DK, 128, FH], BF16, kind="Internal")
    wubf_d = nc.dram_tensor("wubf", [DK, 128, FH], BF16, kind="Internal")
    wdbf_d = nc.dram_tensor("wdbf", [FH, D], BF16, kind="Internal")
    if DBG_SLOT0:
        dbg_yt = nc.dram_tensor("dbg_yt", [128, QC, 128], BF16, kind="ExternalOutput")
        dbg_sf = nc.dram_tensor("dbg_sf", [128, D], BF16, kind="ExternalOutput")
        dbg_qs = nc.dram_tensor("dbg_qs", [65, NH, 128], BF16, kind="ExternalOutput")

    es = ExitStack()

    def sbuf(n, s, d):
        return es.enter_context(nc.sbuf_tensor(n, s, d))

    wc_sb = sbuf("wc_sb", [128, DK, LAT], BF16)
    wq_sb = sbuf("wq_sb", [128, LK, D], BF16)
    teff_sb = sbuf("teff_sb", [128, DK, 128], BF16)
    wproj_sb = sbuf("wproj_sb", [128, DK, D], BF16)
    wg_t = sbuf("wg_t", [128, 3, DK, 128], BF16)
    wu_t = sbuf("wu_t", [128, 3, DK, 128], BF16)
    wd_t = sbuf("wd_t", [128, 6, 512], BF16)
    ident_sb = sbuf("ident_sb", [128, 128], BF16)
    ident2_sb = sbuf("ident2_sb", [128, HD], BF16)
    masks_sb = sbuf("masks_sb", [128, 4, 128], BF16)
    xt = sbuf("xt", [128, 2, D], F32)
    stat = sbuf("stat", [128, 24], F32)
    stat_i = sbuf("stat_i", [128, 2], I32)
    xn_t = sbuf("xn_t", [128, 4, D], BF16)
    sel_sb = sbuf("sel_sb", [128, 4], F32)
    xnT = sbuf("xnT", [128, 1, DK, 256], BF16)
    xnT_own = sbuf("xnT_own", [128, DK, OWN], BF16)
    kT_sb = sbuf("kT_sb", [128, T], BF16)
    vaug_sb = sbuf("vaug_sb", [128, J, HD + 1], BF16)
    vtmp = sbuf("vtmp", [128, 256], BF16)
    sc_bf = sbuf("sc_bf", [128, 512], BF16)
    latT = sbuf("latT", [128, LK, 128], BF16)
    qtmp = sbuf("qtmp", [128, QC, 128], BF16)
    qs = sbuf("qs", [128, 2, NH, 128], BF16)
    p_sb = sbuf("p_sb", [128, 2, 1024], BF16)
    yf = sbuf("yf", [128, HGW], BF16)
    rec = sbuf("rec", [128, HGW], F32)
    recbf = sbuf("recbf", [128, HGW], BF16)
    ytmp = sbuf("ytmp", [128, HGW], BF16)
    yT = sbuf("yT", [128, 2, QC, 128], BF16)
    s_f = sbuf("s_f", [128, 4, D], BF16)
    h_sb = sbuf("h_sb", [128, FM, 256], BF16)
    th1 = sbuf("th1", [128, 256], BF16)
    t_sb = sbuf("t_sb", [128, 256], F32)
    t2_sb = sbuf("t2_sb", [128, D], BF16)
    u1_sb = sbuf("u1_sb", [128, D], BF16)
    u2_sb = sbuf("u2_sb", [128, D], F32)
    o_sb = sbuf("o_sb", [128, 2, D], F32)
    xo_t = sbuf("xo_t", [128, 2, D], F32)
    ones_sb = sbuf("ones_sb", [128, 4], BF16)
    b64_sb = sbuf("b64_sb", [128, 65], BF16)
    onesr_sb = sbuf("onesr_sb", [128, 64], BF16)
    q2m = sbuf("q2m", [128, 8], F32)
    q2m_i = sbuf("q2m_i", [128, 1], I32)
    mrow = sbuf("mrow", [128, max(S, 2)], F32)
    cst = sbuf("cst", [128, 4], F32)

    need = LK * D + LK * 2 * HD
    if FM * 256 >= need:
        h_flat = h_sb[:, :, :].rearrange("p a b -> p (a b)")
        wct_v = h_flat[:, 0:LK * D].rearrange("p (k m) -> p k m", k=LK)
        wkv_v = h_flat[:, LK * D:need].rearrange("p (k n) -> p k n", k=LK)
    else:
        wct_v = sbuf("wct_sb", [128, LK, D], BF16)[:, :, :]
        wkv_v = sbuf("wkv_sb", [128, LK, 2 * HD], BF16)[:, :, :]

    ps_sc = es.enter_context(nc.psum_tensor("ps_sc", [128, 2048], F32))
    ps_y = es.enter_context(nc.psum_tensor("ps_y", [128, 1024], F32))
    ps_u = es.enter_context(nc.psum_tensor("ps_u", [128, 1024], F32))

    with es:
        p = Prog(nc, es)
        O = p.op

        O("gpsimd", lambda e: e.dma_start(
            wc_sb[:, :, :], wc_d.ap().rearrange("(k p) l -> p k l", p=128)), out="wA", dma=True)
        O("gpsimd", lambda e: e.dma_start(
            wq_sb[:, :, :], wq_d.ap().rearrange("(k p) m -> p k m", p=128)), out="wB", dma=True)
        tw_ct = O("gpsimd", lambda e: e.dma_start(
            wct_v, wct_d.ap().rearrange("(k p) m -> p k m", p=128)), out="wC", dma=True)
        tw_kv = O("gpsimd", lambda e: e.dma_start(
            wkv_v, wkv_d.ap().rearrange("(k p) n -> p k n", p=128)), out="wD", dma=True)
        O("gpsimd", lambda e: e.dma_start(
            wproj_sb[:, :, :], wproj_d.ap().rearrange("(k p) n -> p k n", p=128)), out="wE", dma=True)
        O("sync", lambda e: e.dma_start(ident_sb[:, :], ident_d[:, :]), out="wF", dma=True)
        O("sync", lambda e: e.dma_start(ident2_sb[:, :], ident2_d[:, :]), out="wF", dma=True)
        O("sync", lambda e: e.dma_start(
            masks_sb[:, :, :], masks_d.ap().rearrange("m p q -> p m q")), out="wF", dma=True)
        O("sync", lambda e: e.dma_start(sel_sb[:, :], sel_d[:, :]), out="wF", dma=True)
        t_misc = p.barrier_tok("wF")
        t_wA = p.barrier_tok("wA")
        t_wB = p.barrier_tok("wB")
        t_wE = p.barrier_tok("wE")

        O("vector", lambda e: e.memset(ones_sb[0:64, 0:1], 1.0), out="v")
        tb64 = O("vector", lambda e: e.memset(b64_sb[:, :], 0.0), out="v")
        O("vector", lambda e: e.memset(b64_sb[:, 64:65], 1.0), [tb64], out="v")
        O("vector", lambda e: e.memset(ones_sb[:, 1:2], 1.0), out="v")
        O("vector", lambda e: e.memset(ones_sb[64:65, 2:3], 1.0), out="v")
        O("vector", lambda e: e.memset(onesr_sb[64:65, :], 1.0), out="v")
        O("vector", lambda e: e.memset(kT_sb[64:65, :], -1.0), out="v")
        O("vector", lambda e: e.memset(vaug_sb[:, :, HD:HD + 1], 1.0), out="v")
        O("vector", lambda e: e.memset(stat[:, 20:22], 0.0), out="v")
        O("vector", lambda e: e.memset(cst[:, 0:1], SIG / 2.0), out="v")
        O("vector", lambda e: e.memset(cst[:, 1:2], 0.25 * SIG ** 3), out="v")
        t_consts = p.barrier_tok("v")

        t_pre = O("scalar", lambda e: e.preload_act_table(AF.Exp), out="sc")

        ftoks = []
        for m in range(DK):
            for k in range(LK):
                ftoks.append(O("tensor", lambda e, m=m, k=k: e.matmul(
                    ps_u[:, m * 128:(m + 1) * 128],
                    wct_v[:, k, m * 128:(m + 1) * 128],
                    wkv_v[:, k, :],
                    start=(k == 0), stop=(k == LK - 1)), [tw_ct, tw_kv], out="mm"))
        t_teff = O("vector", lambda e: e.tensor_copy(
            teff_sb[:, :, :], ps_u[:, :].rearrange("p (k n) -> p k n", k=DK)),
            [ftoks[-1], t_consts], out="v")

        st = {
            "util": t_teff, "ln": t_consts, "lnsum": [t_consts, t_consts], "lnmax": t_consts, "sq": t_pre, "exp": None, "yfree": (),
            "x_war": [None, None], "xo_war": [None, None], "xn_war": [None] * 4, "xnown_war": None,
            "xnacc": [None, None],
            "xnT_war": None, "p_war": [None, None], "yT_war": [None, None],
            "o_war": [None, None], "qs_war": None, "hbuf_war": t_teff,
            "wg_war": [None, None, None], "wu_war": [None, None, None], "wd_war": [None] * 6,
            "vt_war": t_consts, "scbf_war": None, "latq_war": None,
            "ytmp_war": (), "yf_war": None, "rec_war": None,
            "t2_war": None, "u1_war": None, "u2_war": None, "tsb_war": None, "recbf_war": None,
            "sc_war": [None, None], "qs_war2": [None, None], "q2m_war": None,
            "th1_war": None,
        }
        kv_done = [None] * NB2
        q_done = [None] * S
        aug_done = [None] * S
        xown_done = [None] * S
        sf_done = [None] * 4
        sf_war = [None] * 4
        dn_free = [None, None]
        latq_ctx = {}
        latq_ctx2 = {}

        def newton_rsqrt_ap(wait, v_ap, out_ap, yi_ap, t_ap, iters=3, waits=None):
            ws = list(waits) if waits else [wait]
            a = O("vector", lambda e: e.tensor_single_scalar(
                yi_ap, v_ap.bitcast(I32), 1, ALU.arith_shift_right), ws, out="v")
            a = O("vector", lambda e: e.tensor_scalar(
                yi_ap, yi_ap, MAGIC, -1, ALU.subtract, ALU.mult), [a], out="v")
            yfp = yi_ap.bitcast(F32)
            for it in range(iters):
                a = O("vector", lambda e: e.tensor_tensor(t_ap, yfp, yfp, ALU.mult), [a], out="v")
                a = O("vector", lambda e: e.tensor_tensor(t_ap, t_ap, v_ap, ALU.mult), [a], out="v")
                a = O("vector", lambda e: e.tensor_scalar(t_ap, t_ap, -0.5, 1.5,
                                                          ALU.mult, ALU.add), [a], out="v")
                dst = yfp if it < iters - 1 else out_ap
                a = O("vector", lambda e, dst=dst: e.tensor_tensor(dst, yfp, t_ap, ALU.mult),
                      [a], out="v")
            return a

        ln_pend = []
        ln_war = [None, None]

        def emit_ln_stats_a(x_ap, waits, gi):
            a = O("vector", lambda e: e.reduce_sum(stat[:, 16 + gi:17 + gi], x_ap,
                                                   axis=AX.X),
                  [st["lnsum"][gi]] + waits, out="v")
            a = O("vector", lambda e, gi=gi: e.tensor_scalar_mul(
                stat[:, gi:gi + 1], stat[:, 16 + gi:17 + gi], -1.0 / D),
                [a, ln_war[gi]], out="v")
            return a

        def emit_ln_stats_b(x_ap, waits, gi, dump_ap, a):
            sq = O("scalar", lambda e, gi=gi, dump=dump_ap: e.activation(
                dump, x_ap, AF.Square, bias=stat[:, gi:gi + 1],
                accum_out=stat[:, 18 + gi:19 + gi]), [a, st["sq"]] + waits, out="sc")
            b = O("vector", lambda e, gi=gi: e.tensor_scalar(
                stat[:, 2 + gi:3 + gi], stat[:, 18 + gi:19 + gi], 1.0 / D, EPS,
                ALU.mult, ALU.add), [sq, ln_war[gi]], out="v")
            st["lnsum"][gi] = b
            st["sq"] = sq
            return b

        def ln_flush():
            if not ln_pend:
                return
            n = len(ln_pend)
            bt = [emit_ln_stats_b(x[0], x[4], x[3], x[1], x[7]) for x in ln_pend]
            b = newton_rsqrt_ap(None, stat[:, 2:2 + n],
                                stat[:, 4:4 + n], stat_i[:, 0:n], stat[:, 6:6 + n],
                                iters=2, waits=bt)
            st["ln"] = b
            for x_ap, xn_out_ap, dst_ap, gi, waits, done_aff, done_tp, a, acc in ln_pend:
                aff = O("vector", lambda e, x_ap=x_ap, xn_out_ap=xn_out_ap, gi=gi:
                        e.tensor_scalar(xn_out_ap, x_ap, stat[:, gi:gi + 1],
                                        stat[:, 4 + gi:5 + gi], ALU.add, ALU.mult),
                        [b] + waits, out="v")
                ln_war[gi] = aff
                done_aff(aff)

                tq_must.append(lambda aff=aff, xn_out_ap=xn_out_ap, dst_ap=dst_ap,
                               done_tp=done_tp, acc=acc: gen_transposes(
                                   xn_out_ap, dst_ap, [aff], done_tp, acc))
            ln_pend.clear()

        def submit_ln(x_ap, xn_out_ap, dst_ap, waits, done_aff, done_tp,
                      flush=False, acc=None):
            gi = len(ln_pend)
            a = emit_ln_stats_a(x_ap, waits, gi)
            ln_pend.append((x_ap, xn_out_ap, dst_ap, gi, waits, done_aff,
                            done_tp, a, acc))
            if flush or len(ln_pend) == 2:
                ln_flush()

        def emit_transposes(xn_ap, dst_ap, deps):
            mts = []
            for c in range(DK):
                mts.append(O("tensor", lambda e, c=c: e.matmul(
                    ps_u[:, c * 128:(c + 1) * 128],
                    xn_ap[:, c * 128:(c + 1) * 128],
                    ident_sb[:, :], start=True, stop=True),
                    deps + [t_misc, st["util"]], out="mm"))
            ev = O("vector", lambda e: e.tensor_copy(
                dst_ap, ps_u[:, :].rearrange("p (k n) -> p k n", k=DK)),
                [mts[-1]], out="v")
            st["util"] = ev
            return ev

        def gen_transposes(xn_ap, dst_ap, deps, done_tp, acc=None):
            mts = []
            for c in range(DK):
                mts.append(O("tensor", lambda e, c=c: e.matmul(
                    ps_u[:, c * 128:(c + 1) * 128],
                    xn_ap[:, c * 128:(c + 1) * 128],
                    ident_sb[:, :], start=True, stop=True),
                    deps + [t_misc, st["util"]], out="mm"))
                if YIELDS and "tp" in YSET and c % 4 == 3 and c < DK - 1:
                    yield
            ev = O("vector", lambda e: e.tensor_copy(
                dst_ap, ps_u[:, :].rearrange("p (k n) -> p k n", k=DK)),
                [mts[-1]], out="v")
            st["util"] = ev
            if acc is not None:
                q, g = acc
                dstv = xnT_own[:, :, g * 128:(g + 1) * 128]
                if q == 0:
                    a2 = O("vector", lambda e, dstv=dstv, srcv=dst_ap, q=q:
                           e.tensor_scalar_mul(dstv, srcv, sel_sb[:, q:q + 1]),
                           [ev, st["xnown_war"], t_misc], out="v")
                else:
                    a2 = O("vector", lambda e, dstv=dstv, srcv=dst_ap, q=q:
                           e.scalar_tensor_tensor(dstv, srcv, sel_sb[:, q:q + 1],
                                                  dstv, ALU.mult, ALU.add),
                           [ev, st["xnown_war"], t_misc], out="v")
                st["xnown_war"] = a2
                st["xnacc"][q % 2] = a2
                if q == 3:
                    xown_done[g] = a2
            done_tp(ev)

        def emit_batch_tile(j, flush=False):
            par = j % 2
            dma = O("sync", lambda e, j=j, par=par: e.dma_start(
                xt[:, par, :], xb_d[j * 128:(j + 1) * 128, :]),
                [st["x_war"][par]], out=f"x{par}", dma=True)
            xpar = j % 4

            def done_aff(aff, par=par):
                st["x_war"][par] = aff

            def done_tp(tp, xpar=xpar):
                st["xn_war"][xpar] = tp

            submit_ln(xt[:, par, :], xn_t[:, xpar, :],
                      xnT[:, 0, :, (j % 2) * 128:(j % 2) * 128 + 128],
                      [dma, st["xn_war"][xpar], st["xnT_war"], st["xnacc"][j % 2]],
                      done_aff, done_tp, flush=flush, acc=(j % 4, j // 4))

        def gen_kv_block(b2):
            col0 = b2 * 256
            kvm = []
            for k in range(DK):
                kvm.append(O("tensor", lambda e, k=k: e.matmul(
                    ps_u[:, 0:256], teff_sb[:, k, :], xnT[:, 0, k, :],
                    start=(k == 0), stop=(k == DK - 1)),
                    [st["util"], t_wA], out="mm"))
                if YIELDS and "kv" in YSET and k % 4 == 3 and k < DK - 1:
                    yield
            st["xnT_war"] = kvm[-1]
            kev = O("vector", lambda e, col0=col0: e.tensor_copy(
                kT_sb[0:64, col0:col0 + 256], ps_u[0:64, 0:256]), [kvm[-1]], out="v")
            vev = O("vector", lambda e: e.tensor_copy(
                vtmp[64:128, :], ps_u[64:128, 0:256]), [kvm[-1], st["vt_war"]], out="v")
            if YIELDS and "kv" in YSET:
                yield
            vts = []
            for tt in range(2):
                vts.append(O("tensor", lambda e, tt=tt: e.matmul(
                    ps_u[:, tt * 64:(tt + 1) * 64],
                    vtmp[64:128, tt * 128:(tt + 1) * 128],
                    ident2_sb[64:128, 0:HD], start=True, stop=True),
                    [vev, kev, t_misc], out="mm"))
            st["vt_war"] = vts[-1]
            vaev = O("vector", lambda e, b2=b2: e.tensor_copy(
                vaug_sb[:, 2 * b2:2 * b2 + 2, 0:HD],
                ps_u[:, 0:128].rearrange("p (a b) -> p a b", a=2)),
                [vts[-1]], out="v")
            sqk = O("vector", lambda e, col0=col0: e.tensor_tensor(
                sc_bf[0:64, 0:256], kT_sb[0:64, col0:col0 + 256],
                kT_sb[0:64, col0:col0 + 256], ALU.mult), [kev, st["scbf_war"]], out="v")
            k2m = O("tensor", lambda e: e.matmul(
                ps_u[0:65, 128:384], b64_sb[0:64, 0:65], sc_bf[0:64, 0:256],
                start=True, stop=True), [sqk, vaev, t_consts], out="mm")
            st["scbf_war"] = k2m
            r1 = O("vector", lambda e: e.reduce_max(stat[64:65, 21:22], ps_u[64:65, 128:384],
                                                    axis=AX.X), [k2m, st["lnmax"]], out="v")
            r2 = O("vector", lambda e: e.tensor_tensor(stat[64:65, 20:21], stat[64:65, 20:21],
                                                       stat[64:65, 21:22], ALU.max), [r1], out="v")
            st["lnmax"] = r2
            st["util"] = r1
            kv_done[b2] = p.barrier_tok("v")

        def gen_latq_q(e):
            lats = []
            for m in range(LK):
                for k in range(DK):
                    lats.append(O("tensor", lambda e_, m=m, k=k, e=e: e_.matmul(
                        ps_u[:, m * 128:(m + 1) * 128],
                        wc_sb[:, k, m * 128:(m + 1) * 128],
                        xnT_own[:, k, e * 128:(e + 1) * 128],
                        start=(k == 0), stop=(k == DK - 1)),
                        [xown_done[e], st["util"], st["latq_war"], t_wA], out="mm"))
                if YIELDS and "latqA" in YSET and len(lats) % 8 == 0:
                    yield
            lev = O("vector", lambda e_: e_.tensor_scalar_mul(
                latT[:, :, :], ps_u[:, 0:LK * 128].rearrange("p (k n) -> p k n", k=LK),
                1.0 / np.sqrt(HD)), [lats[-1]], out="v")
            st["util"] = lev
            qms = []
            for m in range(QC):
                for k in range(LK):
                    qms.append(O("tensor", lambda e_, m=m, k=k: e_.matmul(
                        ps_u[:, m * 128:(m + 1) * 128],
                        wq_sb[:, k, m * 128:(m + 1) * 128],
                        latT[:, k, :], start=(k == 0), stop=(k == LK - 1)),
                        [lev, t_wB], out="mm"))
                if YIELDS and "latqB" in YSET and len(qms) % 8 == 0:
                    yield
            qev = O("vector", lambda e_: e_.tensor_copy(
                qtmp[:, :, :], ps_u[:, :].rearrange("p (m n) -> p m n", m=QC)),
                [qms[-1]], out="v")
            st["latq_war"] = qev
            qflat = qtmp[:, :, :].rearrange("p m n -> p (m n)")
            q2toks = []
            for half in range(2):
                qsq = O("vector", lambda e_, half=half: e_.tensor_tensor(
                    sc_bf[:, :], qflat[:, half * 512:(half + 1) * 512],
                    qflat[:, half * 512:(half + 1) * 512], ALU.mult),
                    [qev, st["scbf_war"]], out="v")
                q2toks.append(O("tensor", lambda e_, half=half: e_.matmul(
                    ps_u[0:65, half * 512:(half + 1) * 512], b64_sb[:, 0:65],
                    sc_bf[:, :], start=True, stop=True), [qsq], out="mm"))
                st["scbf_war"] = q2toks[-1]
            q2a, q2b = q2toks
            m1 = O("vector", lambda e_: e_.reduce_max(q2m[64:65, 0:1], ps_u[64:65, 0:1024],
                                                      axis=AX.X), [q2a, q2b, st["q2m_war"]], out="v")
            st["util"] = m1
            qpar = e % 2
            qsv = qs[:, qpar, :, :].rearrange("p h n -> p (h n)") \
                                   .rearrange("p (m two n) -> p m two n", two=2, n=128)
            evn = O("vector", lambda e_, qsv=qsv: e_.tensor_copy(
                qsv[0:64, :, 0, :], qtmp[0:64, :, :]),
                [qev, st["qs_war2"][qpar]], out="v")
            odd = O("sync", lambda e_, qsv=qsv: e_.dma_start(
                qsv[0:64, :, 1, :], qtmp[64:128, :, :]),
                [qev, st["qs_war2"][qpar]], out=f"qo{e % 2}", dma=True)
            q_done[e] = odd
            latq_ctx[e] = (evn,)
            latq_ctx2[e] = m1

        def emit_latq_m(e):
            (evn,) = latq_ctx.pop(e)
            m2 = O("vector", lambda e_: e_.tensor_tensor(q2m[64:65, 1:2], q2m[64:65, 0:1],
                                                         stat[64:65, 20:21], ALU.mult),
                   [st["lnmax"], latq_ctx2.pop(e)], out="v")
            st["lnmax"] = m2
            m3 = newton_rsqrt_ap(m2, q2m[64:65, 1:2], q2m[64:65, 2:3], q2m_i[64:65, 0:1],
                                 q2m[64:65, 3:4], iters=1)
            m4 = O("vector", lambda e_: e_.tensor_tensor(q2m[64:65, 4:5], q2m[64:65, 1:2],
                                                         q2m[64:65, 2:3], ALU.mult),
                   [m3, st["q2m_war"]], out="v")
            qpar = e % 2
            aug = O("vector", lambda e_, e=e, qpar=qpar: e_.tensor_scalar_mul(
                qs[64:65, qpar, :, :].rearrange("p h n -> p (h n)"),
                ones_sb[64:65, 2:3].broadcast_to([1, QS_W]),
                q2m[64:65, 4:5]), [m4, evn, t_consts, st["qs_war2"][qpar]], out="v")
            st["q2m_war"] = aug
            aug_done[e] = aug

        def emit_attn_hg(e, hg, pump, ytl, flush_epi):
            U = 4 * (e + 1)
            co = hg * HGW
            qsflat = qs[:, e % 2, :, :].rearrange("p h n -> p (h n)")
            sc_last = None
            mtoks = {}
            sc_toks = {}

            def emit_av(t):
                par = t % 2
                last = None
                for nh in range(2):
                    yw = list(st["yfree"]) if (t == 0 and st["yfree"]) else []
                    last = O("tensor", lambda e_, t=t, nh=nh, par=par: e_.matmul(
                        ps_y[0:65, nh * 512:(nh + 1) * 512],
                        vaug_sb[:, t, :],
                        p_sb[:, par, nh * 512:(nh + 1) * 512],
                        start=(t == 0), stop=(t == U - 1)),
                        [mtoks[t], kv_done[t // 2]] + yw, out="mm")
                st["p_war"][par] = last
                return last

            for t in range(U):
                sb2 = t % 2
                sct = None
                for nh in range(2):
                    sct = O("tensor", lambda e_, t=t, nh=nh, sb2=sb2: e_.matmul(
                        ps_sc[:, sb2 * 1024 + nh * 512:sb2 * 1024 + (nh + 1) * 512],
                        kT_sb[0:65, t * 128:(t + 1) * 128],
                        qsflat[0:65, co + nh * 512:co + (nh + 1) * 512],
                        start=True, stop=True),
                        [st["sc_war"][sb2], kv_done[t // 2], q_done[e], aug_done[e]],
                        out="mm")
                sc_toks[t] = sct
                sc_last = sct
                if t > 0:
                    emit_av(t - 1)
                ex = O("scalar", lambda e_, sb2=sb2: e_.activation(
                    p_sb[:, sb2, :], ps_sc[:, sb2 * 1024:(sb2 + 1) * 1024], AF.Exp),
                    [sct, st["p_war"][sb2]], out="sc")
                st["sc_war"][sb2] = ex
                st["exp"] = ex
                if t >= U - 4:
                    mj = t - (U - 4)
                    pview = p_sb[:, sb2, :].rearrange("p (h n) -> p h n", n=128)
                    mtoks[t] = O("gpsimd", lambda e_, pview=pview, mj=mj: e_.tensor_tensor(
                        pview, pview,
                        masks_sb[:, mj, :].unsqueeze(1).broadcast_to([128, NHG, 128]),
                        ALU.mult), [ex, t_misc], out="g")
                else:
                    mtoks[t] = ex
                pump()
            av_last = emit_av(U - 1)
            st["qs_war2"][e % 2] = sc_last
            flush_epi()
            yev = O("vector", lambda e_: e_.tensor_copy(yf[0:64, :], ps_y[0:64, :]),
                    [av_last, st["yf_war"]], out="v")
            dcp = O("scalar", lambda e_: e_.activation(rec[64:65, :], ps_y[64:65, :],
                                                       AF.Copy), [av_last, st["rec_war"]],
                    out="sc")
            st["yfree"] = (yev, dcp)
            st["rec_war"] = dcp

            def _recip(e_):
                with nc.allow_low_precision(reason="recip feeds a bf16 matmul operand"):
                    return e_.reciprocal(recbf[64:65, :], rec[64:65, :])
            rcb = O("vector", _recip, [dcp, st["recbf_war"]], out="v")
            yf_tok = yev
            spar = e % 2

            def epilogue():
                rmm = []
                for nh in range(2):
                    rmm.append(O("tensor", lambda e_, nh=nh: e_.matmul(
                        ps_u[0:64, nh * 512:(nh + 1) * 512], onesr_sb[64:65, :],
                        recbf[64:65, nh * 512:(nh + 1) * 512],
                        start=True, stop=True), [rcb, t_consts, st["util"]], out="mm"))
                yn = O("vector", lambda e_: e_.tensor_tensor(
                    ytmp[0:64, :], yf[0:64, :], ps_u[0:64, :], ALU.mult),
                    [yf_tok, rmm[-1]] + list(st["ytmp_war"]), out="v")
                st["yf_war"] = yn
                st["recbf_war"] = rmm[-1]
                st["util"] = yn
                ytv = ytmp[:, :].rearrange("p (m two n) -> p m two n", two=2, n=128)
                pe_ = O("vector", lambda e_, spar=spar, hg=hg: e_.tensor_copy(
                    yT[0:64, spar, hg * 4:(hg + 1) * 4, :], ytv[0:64, :, 0, :]),
                    [yn, st["yT_war"][spar]], out="v")
                po_ = O("sync", lambda e_, spar=spar, hg=hg: e_.dma_start(
                    yT[64:128, spar, hg * 4:(hg + 1) * 4, :], ytv[0:64, :, 1, :]),
                    [yn, st["yT_war"][spar]], out=f"yo{spar}", dma=True)
                st["ytmp_war"] = (pe_, po_)
                ytl.append(pe_)
                ytl.append(po_)

            return epilogue

        def gen_proj(e, ytoks):
            while YIELDS and e >= 4 and (e - 4) not in combine_emitted:
                yield "stall"
            spar = e % 2
            pms = []
            for k in range(QC):
                for nh in range(2):
                    pms.append(O("tensor", lambda e_, k=k, nh=nh, spar=spar: e_.matmul(
                        ps_u[:, nh * 512:(nh + 1) * 512],
                        yT[:, spar, k, :],
                        wproj_sb[:, k, nh * 512:(nh + 1) * 512],
                        start=(k == 0), stop=(k == QC - 1)),
                        ytoks + [st["util"], t_wE], out="mm"))
                if YIELDS and "proj" in YSET and k % 2 == 1 and k < QC - 1:
                    yield
            sev = O("vector", lambda e_, e=e: e_.tensor_copy(s_f[:, e % 4, :], ps_u[:, :]),
                    [pms[-1], sf_war[e % 4]], out="v")
            st["util"] = sev
            st["yT_war"][spar] = pms[-1]
            sf_done[e % 4] = sev
            proj_emitted.add(e)
            if DBG_SLOT0 and e == 0:
                d1 = O("sync", lambda e_: e_.dma_start(dbg_yt[:, :, :], yT[:, 0, :, :]),
                       [pms[-1]] + ytoks, out="o0", dma=True)
                d2 = O("sync", lambda e_: e_.dma_start(dbg_sf[:, :], s_f[:, 0, :]),
                       [sev], out="o0", dma=True)
                d3 = O("sync", lambda e_: e_.dma_start(dbg_qs[:, :, :], qs[0:65, 0, :, :]),
                       [sev, aug_done[0], q_done[0]], out="o0", dma=True)
                st["yT_war"][spar] = d1
                sf_war[0] = d2
                st["qs_war2"][0] = d3

        wg_wb = {}
        wu_wb = {}
        wd_wb = {}
        wb_ring = {"g": [None] * 3, "u": [None] * 3, "d": [None] * 6}
        wb_chain = {}
        proj_emitted = set()
        combine_emitted = set()

        def emit_wb(tag, fn, dep):
            t = O("sync", fn, [dep, wb_chain.get(tag)], out=tag, dma=True)
            wb_chain[tag] = t
            return t

        def load_gu(m, par, first_wb, from_bf):
            gsrc = (wgbf_d if from_bf else wg_d)
            usrc = (wubf_d if from_bf else wu_d)
            eng = "sync" if from_bf else "gpsimd"
            tg = O(eng, lambda e_, m=m, par=par, gsrc=gsrc: e_.dma_start(
                wg_t[:, par, :, :],
                gsrc[:, :, m * 128:(m + 1) * 128].rearrange("k p c -> p k c")),
                [st["wg_war"][par], wb_ring["g"][par],
                 wg_wb.get(m) if from_bf else None],
                out=(f"sg{par}" if from_bf else f"fg{par}"), dma=True)
            tu = O(eng, lambda e_, m=m, par=par, usrc=usrc: e_.dma_start(
                wu_t[:, par, :, :],
                usrc[:, :, m * 128:(m + 1) * 128].rearrange("k p c -> p k c")),
                [st["wu_war"][par], wb_ring["u"][par],
                 wu_wb.get(m) if from_bf else None],
                out=(f"su{par}" if from_bf else f"fu{par}"), dma=True)
            if first_wb:
                wbg = emit_wb(f"wb{m % 2}", lambda e_, m=m, par=par: e_.dma_start(
                    wgbf_d[:, :, m * 128:(m + 1) * 128].rearrange("k p c -> p k c"),
                    wg_t[:, par, :, :]), tg)
                wg_wb[m] = wbg
                wb_ring["g"][par] = wbg
                wbu = emit_wb(f"wb{2 + m % 2}", lambda e_, m=m, par=par: e_.dma_start(
                    wubf_d[:, :, m * 128:(m + 1) * 128].rearrange("k p c -> p k c"),
                    wu_t[:, par, :, :]), tu)
                wu_wb[m] = wbu
                wb_ring["u"][par] = wbu
            return tg, tu

        def gen_ffn_half(B, half):
            tok0 = 4 * B * 128 + half * 256
            own_deps = [xown_done[4 * B + 2 * half + q] for q in range(2)]
            first_wb = (B == 0 and half == 0)
            from_bf = (B > 0)
            pend = {m0: load_gu(m0, m0, first_wb, from_bf) for m0 in range(3)}
            if YIELDS:
                yield
            h2_toks = []
            for m in range(FM):
                par = m % 3
                tg, tu = pend.pop(m)
                gm = None
                for k in range(DK):
                    gm = O("tensor", lambda e_, k=k, par=par, tok0=tok0: e_.matmul(
                        ps_u[:, 0:256], wg_t[:, par, k, :],
                        xnT_own[:, k, tok0:tok0 + 256],
                        start=(k == 0), stop=(k == DK - 1)),
                        [tg, st["util"], st["hbuf_war"]] + own_deps, out="mm")
                um = None
                for k in range(DK):
                    um = O("tensor", lambda e_, k=k, par=par, tok0=tok0: e_.matmul(
                        ps_u[:, 512:768], wu_t[:, par, k, :],
                        xnT_own[:, k, tok0:tok0 + 256],
                        start=(k == 0), stop=(k == DK - 1)), [tu], out="mm")
                tha = O("scalar", lambda e_: e_.activation(
                    t_sb[:, :], ps_u[:, 0:256], AF.Tanh, scale=cst[:, 0:1]),
                    [gm, t_consts, st["tsb_war"]], out="sc")
                h1 = O("vector", lambda e_: e_.scalar_tensor_tensor(
                    th1[:, :], t_sb[:, :], 1.0, ps_u[:, 0:256], ALU.add, ALU.mult),
                    [tha, st["th1_war"]], out="v")
                st["tsb_war"] = h1
                h2 = O("vector", lambda e_, m=m: e_.tensor_tensor(
                    h_sb[:, m, :], th1[:, :], ps_u[:, 512:768], ALU.mult),
                    [h1, um], out="v")
                st["th1_war"] = h2
                st["util"] = h2
                st["wg_war"][par] = gm
                st["wu_war"][par] = um
                h2_toks.append(h2)
                if m + 3 < FM:
                    pend[m + 3] = load_gu(m + 3, par, first_wb, from_bf)
                if YIELDS:
                    yield
            GM = 6
            accs = []
            for tc in range(2):
                eidx = 4 * B + 2 * half + tc
                accs.append((eidx, eidx % 2))
            dlast = None

            def wd_load(seq):
                nh_, m_ = divmod(seq, FM)
                sl_ = seq % 6
                src = (wdbf_d if from_bf else wd_d)
                t = O("sync" if from_bf else "gpsimd",
                      lambda e_, m_=m_, nh_=nh_, sl_=sl_, src=src:
                      e_.dma_start(
                    wd_t[:, sl_, :],
                    src[m_ * 128:(m_ + 1) * 128, nh_ * 512:(nh_ + 1) * 512]),
                    [st["wd_war"][sl_], wb_ring["d"][sl_],
                     wd_wb.get(seq) if from_bf else None],
                    out=(f"sd{sl_}" if from_bf else f"fd{sl_}"), dma=True)
                if first_wb:
                    wb = emit_wb(f"wb{seq % 4}",
                                 lambda e_, m_=m_, nh_=nh_, sl_=sl_: e_.dma_start(
                        wdbf_d[m_ * 128:(m_ + 1) * 128, nh_ * 512:(nh_ + 1) * 512],
                        wd_t[:, sl_, :]), t)
                    wd_wb[seq] = wb
                    wb_ring["d"][sl_] = wb
                return t

            wd_toks = {}
            for s0 in range(6):
                wd_toks[s0] = wd_load(s0)
            for nh in range(2):
                for g4 in range(FM // GM):
                    gtoks = []
                    for mi in range(GM):
                        m = g4 * GM + mi
                        seq = nh * FM + m
                        sl_ = seq % 6
                        tdm = wd_toks.pop(seq)
                        for tc in range(2):
                            dlast = O("tensor", lambda e_, m=m, tc=tc, sl_=sl_,
                                      mi=mi, g4=g4: e_.matmul(
                                ps_u[:, tc * 512:(tc + 1) * 512],
                                h_sb[:, m, tc * 128:(tc + 1) * 128],
                                wd_t[:, sl_, :],
                                start=(mi == 0), stop=(mi == GM - 1)),
                                [tdm, h2_toks[m], st["util"] if mi == 0 else None],
                                out="mm")
                        st["wd_war"][sl_] = dlast
                        if seq + 6 < 2 * FM:
                            wd_toks[seq + 6] = wd_load(seq + 6)
                    for tc in range(2):
                        eidx, par = accs[tc]
                        dst = o_sb[:, par, nh * 512:(nh + 1) * 512]
                        if g4 == 0:
                            ev = O("vector", lambda e_, dst=dst, tc=tc: e_.tensor_copy(
                                dst, ps_u[:, tc * 512:(tc + 1) * 512]),
                                [dlast, st["o_war"][par]], out="v")
                        else:
                            ev = O("vector", lambda e_, dst=dst, tc=tc: e_.tensor_tensor(
                                dst, dst, ps_u[:, tc * 512:(tc + 1) * 512], ALU.add),
                                [dlast], out="v")
                        gtoks.append(ev)
                    st["util"] = gtoks[-1]
                    if YIELDS:
                        yield
            for tc in range(2):
                eidx, par = accs[tc]
                sfi = 2 * half + tc
                while YIELDS and eidx not in proj_emitted:
                    yield "stall"
                facc = o_sb[:, par, :]
                xod = O("sync", lambda e_, eidx=eidx, par=par: e_.dma_start(
                    xo_t[:, par, :], xo_d[eidx * 128:(eidx + 1) * 128, :]),
                    [st["xo_war"][par]], out=f"xo{par}", dma=True)
                t2 = O("scalar", lambda e_, facc=facc: e_.activation(
                    t2_sb[:, :], facc, AF.Tanh, scale=cst[:, 1:2]),
                    [st["util"], t_consts, st["t2_war"]], out="sc")
                u1 = O("vector", lambda e_, facc=facc: e_.scalar_tensor_tensor(
                    u1_sb[:, :], t2_sb[:, :], 1.0, facc, ALU.add, ALU.mult),
                    [t2, st["u1_war"]], out="v")
                st["t2_war"] = u1
                u2 = O("vector", lambda e_, sfi=sfi: e_.scalar_tensor_tensor(
                    u2_sb[:, :], s_f[:, sfi, :], 0.25 * SIG ** 3, u1_sb[:, :],
                    ALU.mult, ALU.mult), [u1, sf_done[sfi], st["u2_war"]], out="v")
                st["u1_war"] = u2
                oc = O("vector", lambda e_, par=par: e_.tensor_tensor(
                    o_sb[:, par, :], u2_sb[:, :], xo_t[:, par, :], ALU.add),
                    [u2, xod, u1], out="v")
                od = O("sync", lambda e_, eidx=eidx, par=par: e_.dma_start(
                    out_d[eidx * 128:(eidx + 1) * 128, :], o_sb[:, par, :]),
                    [oc], out=f"o{par}", dma=True)
                st["u2_war"] = oc
                st["o_war"][par] = od
                st["xo_war"][par] = oc
                sf_war[sfi] = u2
                combine_emitted.add(eidx)
                if YIELDS:
                    yield
            st["hbuf_war"] = dlast

        from collections import deque

        vecq = deque()
        tq_must = deque()
        tq_lazy = deque()
        epiq = deque()
        act_m = [None]
        act_l = [None]
        tick = [0]
        EPI_DELAY = 12

        LAZY_STEPS = 1

        def run_all(item):
            r = item()
            if r is not None:
                for _ in r:
                    pass

        def adv(gen, steps, stash=None):
            for _ in range(steps):
                try:
                    v = next(gen)
                except StopIteration:
                    return "done"
                if v == "stall":
                    if stash is not None:
                        stash.appendleft(gen)
                    return "stall"
            return "active"

        def finish_must():
            if act_m[0] is not None:
                for _ in act_m[0]:
                    pass
                act_m[0] = None

        def lazy_step():
            if act_l[0] is not None:
                r = adv(act_l[0], LAZY_STEPS)
                if r == "stall":
                    tq_lazy.appendleft(act_l[0])
                    act_l[0] = None
                elif r == "done":
                    act_l[0] = None
                return True
            if tq_lazy:
                ent = tq_lazy.popleft()
                if callable(ent):
                    ent = ent()
                    if ent is None:
                        return True
                r = adv(ent, 1)
                if r == "active":
                    act_l[0] = ent
                elif r == "stall":
                    tq_lazy.appendleft(ent)
                return True
            return False

        def lazy_step_force():
            if not lazy_step():
                raise RuntimeError("stalled with no lazy work")

        def enqueue_group(g):
            def pair1(g=g):
                emit_batch_tile(4 * g)
                emit_batch_tile(4 * g + 1)
                tq_must.append(lambda: gen_kv_block(2 * g))
            def pair2(g=g):
                emit_batch_tile(4 * g + 2)
                emit_batch_tile(4 * g + 3)
                tq_must.append(lambda: gen_kv_block(2 * g + 1))
            def tail(g=g):
                tq_must.append(lambda: gen_latq_q(g))
                tq_must.append(lambda: (emit_latq_m(g), None)[1])
            vecq.append(pair1)
            vecq.append(pair2)
            vecq.append(tail)

        def pump():
            tick[0] += 1
            if vecq:
                vecq.popleft()()
            if act_m[0] is not None:
                if adv(act_m[0], 1) != "active":
                    act_m[0] = None
                return
            must_pending = bool(tq_must) or (bool(epiq) and
                                             tick[0] >= epiq[0][0])
            if (act_l[0] is not None or tq_lazy) and \
                    (tick[0] % 2 == 0 or not must_pending):
                lazy_step()
                return
            if tq_must:
                r = tq_must.popleft()()
                if r is not None and adv(r, 1) == "active":
                    act_m[0] = r
                return
            if epiq and tick[0] >= epiq[0][0]:
                rdy, ent = epiq.popleft()
                if callable(ent):
                    ent = ent()
                if ent is None:
                    return
                r = adv(ent, 1)
                if r == "active":
                    act_m[0] = ent
                    return
                if r != "stall":
                    return
                epiq.appendleft((rdy, ent))
                lazy_step()
                return
            lazy_step()

        def drain(q):
            finish_must()
            while q:
                run_all(q.popleft())

        enqueue_group(0)
        while vecq:
            vecq.popleft()()
        drain(tq_must)

        for e in range(S):
            if e + 1 < S:
                enqueue_group(e + 1)
            if e % 2 == 1:
                k = (e - 1) // 2
                tq_lazy.append(lambda B=k // 2, hf=k % 2: gen_ffn_half(B, hf))

            def flush_epi():
                finish_must()
                while epiq:
                    ent = epiq.popleft()[1]
                    if callable(ent):
                        ent = ent()
                    if ent is None:
                        continue
                    while True:
                        r = adv(ent, 1)
                        if r == "done":
                            break
                        if r == "stall":
                            lazy_step_force()

            ytl = []
            for hg in range(2):
                epi = emit_attn_hg(e, hg, pump, ytl, flush_epi)
                epiq.append((tick[0] + EPI_DELAY,
                             lambda epi=epi: (epi(), None)[1]))
            epiq.append((tick[0] + EPI_DELAY, lambda e=e, ytl=ytl: gen_proj(e, ytl)))
            while vecq:
                vecq.popleft()()
            drain(tq_must)

        finish_must()
        while epiq:
            ent = epiq.popleft()[1]
            if callable(ent):
                ent = ent()
            if ent is None:
                continue
            while True:
                r = adv(ent, 1)
                if r == "done":
                    break
                if r == "stall":
                    lazy_step_force()
        while tq_lazy or act_l[0] is not None:
            a = act_l[0]
            if a is not None:
                for _ in a:
                    pass
                act_l[0] = None
                continue
            ent = tq_lazy.popleft()
            if callable(ent):
                run_all(ent)
            else:
                for _ in ent:
                    pass

        O("sync", lambda e_: e_.nop(), [p.barrier_tok("o0"), p.barrier_tok("o1")])

        import os
        if os.environ.get("MK_DEBUG_SEMS"):
            print("SEMS:", {k: v[1] for k, v in p._sems.items()})
        block = es.enter_context(nc.Block())
        p.emit(block)

    return nc


def stage_core_inputs(cfg, c, x, wc, wq, wkv, wproj, wg, wu, wd):
    b, i = c // 4, c % 4
    J, S = cfg.J, cfg.S
    xb = np.ascontiguousarray(x[b])
    own_chunks = [4 * e + 3 - i for e in range(S)]
    xo = np.ascontiguousarray(
        xb.reshape(J, 128, cfg.D)[own_chunks].reshape(cfg.OWN, cfg.D))
    kt_idx = np.arange(128)[:, None]
    qt_idx = np.arange(128)[None, :]
    tri = (qt_idx >= kt_idx).astype(np.float32)
    mk = np.zeros((4, 128, 128), np.float32)
    for jj in range(4):
        if jj < 3 - i:
            mk[jj] = 1.0
        elif jj == 3 - i:
            mk[jj] = tri
    return {
        "xb": xb,
        "xo": xo,
        "wc": wc,
        "wq": wq,
        "wct": np.ascontiguousarray(wc.T),
        "wkv": wkv,
        "wproj": wproj,
        "wg": np.ascontiguousarray(wg.T.reshape(cfg.DK, 128, cfg.FH)),
        "wu": np.ascontiguousarray(wu.T.reshape(cfg.DK, 128, cfg.FH)),
        "wd": np.ascontiguousarray(wd.T),
        "ident": np.eye(128, dtype=ml_dtypes.bfloat16),
        "ident2": np.ascontiguousarray(
            np.concatenate([np.eye(cfg.HD)] * 2, axis=0).astype(ml_dtypes.bfloat16)),
        "masks": mk.astype(ml_dtypes.bfloat16),
        "sel": np.tile(np.eye(4, dtype=np.float32)[3 - i], (128, 1)),
    }


def assemble(cfg, results):
    out = np.empty((2, cfg.T, cfg.D), np.float32)
    for c in range(8):
        b, i = c // 4, c % 4
        o = np.asarray(results[c]["out"]).reshape(cfg.S, 128, cfg.D)
        for e in range(cfg.S):
            ch = 4 * e + 3 - i
            out[b, ch * 128:(ch + 1) * 128, :] = o[e]
    return out


def run(cfg, x, wc, wq, wkv, wproj, wg, wu, wd, nc=None):
    if nc is None:
        nc = build_program(cfg)
    in_maps = [stage_core_inputs(cfg, c, x, wc, wq, wkv, wproj, wg, wu, wd)
               for c in range(8)]
    res = run_bass_kernel_spmd(nc, in_maps, core_ids=list(range(8)))
    return assemble(cfg, res.results)


def kernel(L0_attn_x, L0_attn_Wc_down, L0_attn_Wq_up, L0_attn_Wkv_up,
           L0_attn_Wc_proj, L0_ffn_w_gate, L0_ffn_w_up, L0_ffn_w_down):
    cfg = FULL
    f = lambda a, d: np.ascontiguousarray(np.asarray(a, d))
    x = f(L0_attn_x, np.float32)
    return run(cfg, x, f(L0_attn_Wc_down, np.float32), f(L0_attn_Wq_up, np.float32),
               f(L0_attn_Wkv_up, np.float32), f(L0_attn_Wc_proj, np.float32),
               f(L0_ffn_w_gate, np.int32), f(L0_ffn_w_up, np.int32),
               f(L0_ffn_w_down, np.int32))


# version 32 (final)
# speedup vs baseline: 1.0190x; 1.0190x over previous
from contextlib import ExitStack

import numpy as np
import ml_dtypes
import concourse.bass as bass
import concourse.mybir as mybir
from concourse.bass_utils import run_bass_kernel_spmd

F32 = mybir.dt.float32
BF16 = mybir.dt.bfloat16
I32 = mybir.dt.int32
AF = mybir.ActivationFunctionType
ALU = mybir.AluOpType
AX = mybir.AxisListType

MAGIC = 0x5F3759DF
YIELDS = True
DBG_SLOT0 = False
YSET = {"tp", "kv", "latq", "proj", "h"}
SIG = 1.0 / 127.0
EPS = 1e-5


class Tok:
    __slots__ = ("sem", "val")

    def __init__(self, sem, val):
        self.sem = sem
        self.val = val


class Prog:
    def __init__(self, nc, es):
        self.nc = nc
        self.es = es
        self.items = []
        self._sems = {}
        self._producers = {}

    def _sem(self, name):
        if name not in self._sems:
            h = self.es.enter_context(self.nc.semaphore(name))
            self._sems[name] = [h, 0]
        return self._sems[name]

    def op(self, engine, emit, waits=(), out=None, dma=False):
        tok = None
        inc_sem = None
        inc_by = 0
        if out is not None:
            ent = self._sem(out)
            inc_by = 16 if dma else 1
            ent[1] += inc_by
            tok = Tok(ent[0], ent[1])
            inc_sem = ent[0]
            self._producers.setdefault(id(ent[0]), set()).add(engine)
        ws = tuple(t for t in waits if t is not None)
        self.items.append((engine, emit, ws, inc_sem, inc_by))
        return tok

    def barrier_tok(self, name):
        ent = self._sem(name)
        return Tok(ent[0], ent[1])

    def emit(self, block):
        streams = {}
        for it in self.items:
            streams.setdefault(it[0], []).append(it)

        def mk_body(ops, ename):
            def body(eng):
                seen = {}
                for _, emit, waits, inc_sem, inc_by in ops:
                    per = {}
                    for t in waits:
                        k = id(t.sem)
                        if k not in per or per[k][1] < t.val:
                            per[k] = (t.sem, t.val)
                    for k, (sem, val) in per.items():
                        if seen.get(k, -1) >= val:
                            continue
                        if ename == "tensor" and self._producers.get(k) == {"tensor"}:
                            seen[k] = max(seen.get(k, -1), val)
                            continue
                        eng.wait_ge(sem, val)
                        seen[k] = val
                    ins = emit(eng)
                    if inc_sem is not None:
                        ins.then_inc(inc_sem, inc_by)
            return body

        for name, ops in streams.items():
            getattr(block, name)(mk_body(ops, name))


class Cfg:
    def __init__(self, T=8192, D=1024, NH=16, HD=64, LAT=512, FH=3072):
        self.T, self.D, self.NH, self.HD, self.LAT, self.FH = T, D, NH, HD, LAT, FH
        self.J = T // 128
        assert self.J % 4 == 0
        self.S = self.J // 4
        assert self.S % 4 == 0, "FFN blocks cover 4 slots"
        self.DK = D // 128
        self.LK = LAT // 128
        self.FM = FH // 128
        self.QC = NH * HD // 128
        self.NHG = NH // 2
        self.OWN = self.S * 128
        self.QS_W = NH * 128
        self.HGW = self.NHG * 128


FULL = Cfg()


def build_program(cfg: Cfg):
    nc = bass.Bass("TRN2", target_bir_lowering=False)
    T, D, NH, HD, LAT, FH = cfg.T, cfg.D, cfg.NH, cfg.HD, cfg.LAT, cfg.FH
    J, S, DK, LK, FM, QC = cfg.J, cfg.S, cfg.DK, cfg.LK, cfg.FM, cfg.QC
    NHG, OWN, QS_W, HGW = cfg.NHG, cfg.OWN, cfg.QS_W, cfg.HGW
    NB2 = J // 2

    xb_d = nc.dram_tensor("xb", [T, D], F32, kind="ExternalInput")
    xo_d = nc.dram_tensor("xo", [OWN, D], F32, kind="ExternalInput")
    wc_d = nc.dram_tensor("wc", [D, LAT], F32, kind="ExternalInput")
    wq_d = nc.dram_tensor("wq", [LAT, D], F32, kind="ExternalInput")
    wct_d = nc.dram_tensor("wct", [LAT, D], F32, kind="ExternalInput")
    wkv_d = nc.dram_tensor("wkv", [LAT, 2 * HD], F32, kind="ExternalInput")
    wproj_d = nc.dram_tensor("wproj", [D, D], F32, kind="ExternalInput")
    wg_d = nc.dram_tensor("wg", [DK, 128, FH], I32, kind="ExternalInput")
    wu_d = nc.dram_tensor("wu", [DK, 128, FH], I32, kind="ExternalInput")
    wd_d = nc.dram_tensor("wd", [FH, D], I32, kind="ExternalInput")
    ident_d = nc.dram_tensor("ident", [128, 128], BF16, kind="ExternalInput")
    ident2_d = nc.dram_tensor("ident2", [128, HD], BF16, kind="ExternalInput")
    masks_d = nc.dram_tensor("masks", [4, 128, 128], BF16, kind="ExternalInput")
    sel_d = nc.dram_tensor("sel", [128, 4], F32, kind="ExternalInput")
    out_d = nc.dram_tensor("out", [OWN, D], F32, kind="ExternalOutput")
    wgbf_d = nc.dram_tensor("wgbf", [DK, 128, FH], BF16, kind="Internal")
    wubf_d = nc.dram_tensor("wubf", [DK, 128, FH], BF16, kind="Internal")
    wdbf_d = nc.dram_tensor("wdbf", [FH, D], BF16, kind="Internal")
    if DBG_SLOT0:
        dbg_yt = nc.dram_tensor("dbg_yt", [128, QC, 128], BF16, kind="ExternalOutput")
        dbg_sf = nc.dram_tensor("dbg_sf", [128, D], BF16, kind="ExternalOutput")
        dbg_qs = nc.dram_tensor("dbg_qs", [65, NH, 128], BF16, kind="ExternalOutput")

    es = ExitStack()

    def sbuf(n, s, d):
        return es.enter_context(nc.sbuf_tensor(n, s, d))

    wc_sb = sbuf("wc_sb", [128, DK, LAT], BF16)
    wq_sb = sbuf("wq_sb", [128, LK, D], BF16)
    teff_sb = sbuf("teff_sb", [128, DK, 128], BF16)
    wproj_sb = sbuf("wproj_sb", [128, DK, D], BF16)
    wg_t = sbuf("wg_t", [128, 3, DK, 128], BF16)
    wu_t = sbuf("wu_t", [128, 3, DK, 128], BF16)
    wd_t = sbuf("wd_t", [128, 7, 512], BF16)
    ident_sb = sbuf("ident_sb", [128, 128], BF16)
    ident2_sb = sbuf("ident2_sb", [128, HD], BF16)
    masks_sb = sbuf("masks_sb", [128, 4, 128], BF16)
    xt = sbuf("xt", [128, 2, D], F32)
    stat = sbuf("stat", [128, 24], F32)
    stat_i = sbuf("stat_i", [128, 2], I32)
    xn_t = sbuf("xn_t", [128, 4, D], BF16)
    sel_sb = sbuf("sel_sb", [128, 4], F32)
    xnT = sbuf("xnT", [128, 1, DK, 256], BF16)
    xnT_own = sbuf("xnT_own", [128, DK, OWN], BF16)
    kT_sb = sbuf("kT_sb", [128, T], BF16)
    vaug_sb = sbuf("vaug_sb", [128, J, HD + 1], BF16)
    vtmp = sbuf("vtmp", [128, 256], BF16)
    sc_bf = sbuf("sc_bf", [128, 512], BF16)
    latT = sbuf("latT", [128, LK, 128], BF16)
    qtmp = sbuf("qtmp", [128, QC, 128], BF16)
    qs = sbuf("qs", [128, 2, NH, 128], BF16)
    p_sb = sbuf("p_sb", [128, 2, 1024], BF16)
    yf = sbuf("yf", [128, HGW], BF16)
    rec = sbuf("rec", [128, HGW], F32)
    recbf = sbuf("recbf", [128, HGW], BF16)
    ytmp = sbuf("ytmp", [128, HGW], BF16)
    yT = sbuf("yT", [128, 2, QC, 128], BF16)
    s_f = sbuf("s_f", [128, 4, D], BF16)
    h_sb = sbuf("h_sb", [128, FM, 256], BF16)
    th1 = sbuf("th1", [128, 256], BF16)
    t_sb = sbuf("t_sb", [128, 256], F32)
    t2_sb = sbuf("t2_sb", [128, D], BF16)
    u1_sb = sbuf("u1_sb", [128, D], BF16)
    u2_sb = sbuf("u2_sb", [128, D], F32)
    o_sb = sbuf("o_sb", [128, 2, D], F32)
    xo_t = sbuf("xo_t", [128, 2, D], F32)
    ones_sb = sbuf("ones_sb", [128, 4], BF16)
    b64_sb = sbuf("b64_sb", [128, 65], BF16)
    onesr_sb = sbuf("onesr_sb", [128, 64], BF16)
    q2m = sbuf("q2m", [128, 8], F32)
    q2m_i = sbuf("q2m_i", [128, 1], I32)
    mrow = sbuf("mrow", [128, max(S, 2)], F32)
    cst = sbuf("cst", [128, 4], F32)

    need = LK * D + LK * 2 * HD
    if FM * 256 >= need:
        h_flat = h_sb[:, :, :].rearrange("p a b -> p (a b)")
        wct_v = h_flat[:, 0:LK * D].rearrange("p (k m) -> p k m", k=LK)
        wkv_v = h_flat[:, LK * D:need].rearrange("p (k n) -> p k n", k=LK)
    else:
        wct_v = sbuf("wct_sb", [128, LK, D], BF16)[:, :, :]
        wkv_v = sbuf("wkv_sb", [128, LK, 2 * HD], BF16)[:, :, :]

    ps_sc = es.enter_context(nc.psum_tensor("ps_sc", [128, 2048], F32))
    ps_y = es.enter_context(nc.psum_tensor("ps_y", [128, 1024], F32))
    ps_u = es.enter_context(nc.psum_tensor("ps_u", [128, 1024], F32))

    with es:
        p = Prog(nc, es)
        O = p.op

        O("gpsimd", lambda e: e.dma_start(
            wc_sb[:, :, :], wc_d.ap().rearrange("(k p) l -> p k l", p=128)), out="wA", dma=True)
        O("gpsimd", lambda e: e.dma_start(
            wq_sb[:, :, :], wq_d.ap().rearrange("(k p) m -> p k m", p=128)), out="wB", dma=True)
        tw_ct = O("gpsimd", lambda e: e.dma_start(
            wct_v, wct_d.ap().rearrange("(k p) m -> p k m", p=128)), out="wC", dma=True)
        tw_kv = O("gpsimd", lambda e: e.dma_start(
            wkv_v, wkv_d.ap().rearrange("(k p) n -> p k n", p=128)), out="wD", dma=True)
        O("gpsimd", lambda e: e.dma_start(
            wproj_sb[:, :, :], wproj_d.ap().rearrange("(k p) n -> p k n", p=128)), out="wE", dma=True)
        O("sync", lambda e: e.dma_start(ident_sb[:, :], ident_d[:, :]), out="wF", dma=True)
        O("sync", lambda e: e.dma_start(ident2_sb[:, :], ident2_d[:, :]), out="wF", dma=True)
        O("sync", lambda e: e.dma_start(
            masks_sb[:, :, :], masks_d.ap().rearrange("m p q -> p m q")), out="wF", dma=True)
        O("sync", lambda e: e.dma_start(sel_sb[:, :], sel_d[:, :]), out="wF", dma=True)
        t_misc = p.barrier_tok("wF")
        t_wA = p.barrier_tok("wA")
        t_wB = p.barrier_tok("wB")
        t_wE = p.barrier_tok("wE")

        O("vector", lambda e: e.memset(ones_sb[0:64, 0:1], 1.0), out="v")
        tb64 = O("vector", lambda e: e.memset(b64_sb[:, :], 0.0), out="v")
        O("vector", lambda e: e.memset(b64_sb[:, 64:65], 1.0), [tb64], out="v")
        O("vector", lambda e: e.memset(ones_sb[:, 1:2], 1.0), out="v")
        O("vector", lambda e: e.memset(ones_sb[64:65, 2:3], 1.0), out="v")
        O("vector", lambda e: e.memset(onesr_sb[64:65, :], 1.0), out="v")
        O("vector", lambda e: e.memset(kT_sb[64:65, :], -1.0), out="v")
        O("vector", lambda e: e.memset(vaug_sb[:, :, HD:HD + 1], 1.0), out="v")
        O("vector", lambda e: e.memset(stat[:, 20:22], 0.0), out="v")
        O("vector", lambda e: e.memset(cst[:, 0:1], SIG / 2.0), out="v")
        O("vector", lambda e: e.memset(cst[:, 1:2], 0.25 * SIG ** 3), out="v")
        t_consts = p.barrier_tok("v")

        t_pre = O("scalar", lambda e: e.preload_act_table(AF.Exp), out="sc")

        ftoks = []
        for m in range(DK):
            for k in range(LK):
                ftoks.append(O("tensor", lambda e, m=m, k=k: e.matmul(
                    ps_u[:, m * 128:(m + 1) * 128],
                    wct_v[:, k, m * 128:(m + 1) * 128],
                    wkv_v[:, k, :],
                    start=(k == 0), stop=(k == LK - 1)), [tw_ct, tw_kv], out="mm"))
        t_teff = O("vector", lambda e: e.tensor_copy(
            teff_sb[:, :, :], ps_u[:, :].rearrange("p (k n) -> p k n", k=DK)),
            [ftoks[-1], t_consts], out="v")

        st = {
            "util": t_teff, "ln": t_consts, "lnsum": [t_consts, t_consts], "lnmax": t_consts, "sq": t_pre, "exp": None, "yfree": (),
            "x_war": [None, None], "xo_war": [None, None], "xn_war": [None] * 4, "xnown_war": None,
            "xnacc": [None, None],
            "xnT_war": None, "p_war": [None, None], "yT_war": [None, None],
            "o_war": [None, None], "qs_war": None, "hbuf_war": t_teff,
            "wg_war": [None, None, None], "wu_war": [None, None, None], "wd_war": [None] * 7,
            "vt_war": t_consts, "scbf_war": None, "latq_war": None,
            "ytmp_war": (), "yf_war": None, "rec_war": None,
            "t2_war": None, "u1_war": None, "u2_war": None, "tsb_war": None, "recbf_war": None,
            "sc_war": [None, None], "qs_war2": [None, None], "q2m_war": None,
            "th1_war": None,
        }
        kv_done = [None] * NB2
        q_done = [None] * S
        aug_done = [None] * S
        xown_done = [None] * S
        sf_done = [None] * 4
        sf_war = [None] * 4
        dn_free = [None, None]
        latq_ctx = {}
        latq_ctx2 = {}

        def newton_rsqrt_ap(wait, v_ap, out_ap, yi_ap, t_ap, iters=3, waits=None):
            ws = list(waits) if waits else [wait]
            a = O("vector", lambda e: e.tensor_single_scalar(
                yi_ap, v_ap.bitcast(I32), 1, ALU.arith_shift_right), ws, out="v")
            a = O("vector", lambda e: e.tensor_scalar(
                yi_ap, yi_ap, MAGIC, -1, ALU.subtract, ALU.mult), [a], out="v")
            yfp = yi_ap.bitcast(F32)
            for it in range(iters):
                a = O("vector", lambda e: e.tensor_tensor(t_ap, yfp, yfp, ALU.mult), [a], out="v")
                a = O("vector", lambda e: e.tensor_tensor(t_ap, t_ap, v_ap, ALU.mult), [a], out="v")
                a = O("vector", lambda e: e.tensor_scalar(t_ap, t_ap, -0.5, 1.5,
                                                          ALU.mult, ALU.add), [a], out="v")
                dst = yfp if it < iters - 1 else out_ap
                a = O("vector", lambda e, dst=dst: e.tensor_tensor(dst, yfp, t_ap, ALU.mult),
                      [a], out="v")
            return a

        ln_pend = []
        ln_war = [None, None]

        def emit_ln_stats_a(x_ap, waits, gi):
            a = O("vector", lambda e: e.reduce_sum(stat[:, 16 + gi:17 + gi], x_ap,
                                                   axis=AX.X),
                  [st["lnsum"][gi]] + waits, out="v")
            a = O("vector", lambda e, gi=gi: e.tensor_scalar_mul(
                stat[:, gi:gi + 1], stat[:, 16 + gi:17 + gi], -1.0 / D),
                [a, ln_war[gi]], out="v")
            return a

        def emit_ln_stats_b(x_ap, waits, gi, dump_ap, a):
            sq = O("scalar", lambda e, gi=gi, dump=dump_ap: e.activation(
                dump, x_ap, AF.Square, bias=stat[:, gi:gi + 1],
                accum_out=stat[:, 18 + gi:19 + gi]), [a, st["sq"]] + waits, out="sc")
            b = O("vector", lambda e, gi=gi: e.tensor_scalar(
                stat[:, 2 + gi:3 + gi], stat[:, 18 + gi:19 + gi], 1.0 / D, EPS,
                ALU.mult, ALU.add), [sq, ln_war[gi]], out="v")
            st["lnsum"][gi] = b
            st["sq"] = sq
            return b

        def ln_flush():
            if not ln_pend:
                return
            n = len(ln_pend)
            bt = [emit_ln_stats_b(x[0], x[4], x[3], x[1], x[7]) for x in ln_pend]
            b = newton_rsqrt_ap(None, stat[:, 2:2 + n],
                                stat[:, 4:4 + n], stat_i[:, 0:n], stat[:, 6:6 + n],
                                iters=2, waits=bt)
            st["ln"] = b
            for x_ap, xn_out_ap, dst_ap, gi, waits, done_aff, done_tp, a, acc in ln_pend:
                aff = O("vector", lambda e, x_ap=x_ap, xn_out_ap=xn_out_ap, gi=gi:
                        e.tensor_scalar(xn_out_ap, x_ap, stat[:, gi:gi + 1],
                                        stat[:, 4 + gi:5 + gi], ALU.add, ALU.mult),
                        [b] + waits, out="v")
                ln_war[gi] = aff
                done_aff(aff)

                tq_must.append(lambda aff=aff, xn_out_ap=xn_out_ap, dst_ap=dst_ap,
                               done_tp=done_tp, acc=acc: gen_transposes(
                                   xn_out_ap, dst_ap, [aff], done_tp, acc))
            ln_pend.clear()

        def submit_ln(x_ap, xn_out_ap, dst_ap, waits, done_aff, done_tp,
                      flush=False, acc=None):
            gi = len(ln_pend)
            a = emit_ln_stats_a(x_ap, waits, gi)
            ln_pend.append((x_ap, xn_out_ap, dst_ap, gi, waits, done_aff,
                            done_tp, a, acc))
            if flush or len(ln_pend) == 2:
                ln_flush()

        def emit_transposes(xn_ap, dst_ap, deps):
            mts = []
            for c in range(DK):
                mts.append(O("tensor", lambda e, c=c: e.matmul(
                    ps_u[:, c * 128:(c + 1) * 128],
                    xn_ap[:, c * 128:(c + 1) * 128],
                    ident_sb[:, :], start=True, stop=True),
                    deps + [t_misc, st["util"]], out="mm"))
            ev = O("vector", lambda e: e.tensor_copy(
                dst_ap, ps_u[:, :].rearrange("p (k n) -> p k n", k=DK)),
                [mts[-1]], out="v")
            st["util"] = ev
            return ev

        def gen_transposes(xn_ap, dst_ap, deps, done_tp, acc=None):
            mts = []
            for c in range(DK):
                mts.append(O("tensor", lambda e, c=c: e.matmul(
                    ps_u[:, c * 128:(c + 1) * 128],
                    xn_ap[:, c * 128:(c + 1) * 128],
                    ident_sb[:, :], start=True, stop=True),
                    deps + [t_misc, st["util"]], out="mm"))
                if YIELDS and "tp" in YSET and c % 4 == 3 and c < DK - 1:
                    yield
            ev = O("vector", lambda e: e.tensor_copy(
                dst_ap, ps_u[:, :].rearrange("p (k n) -> p k n", k=DK)),
                [mts[-1]], out="v")
            st["util"] = ev
            if acc is not None:
                q, g = acc
                dstv = xnT_own[:, :, g * 128:(g + 1) * 128]
                if q == 0:
                    a2 = O("vector", lambda e, dstv=dstv, srcv=dst_ap, q=q:
                           e.tensor_scalar_mul(dstv, srcv, sel_sb[:, q:q + 1]),
                           [ev, st["xnown_war"], t_misc], out="v")
                else:
                    a2 = O("vector", lambda e, dstv=dstv, srcv=dst_ap, q=q:
                           e.scalar_tensor_tensor(dstv, srcv, sel_sb[:, q:q + 1],
                                                  dstv, ALU.mult, ALU.add),
                           [ev, st["xnown_war"], t_misc], out="v")
                st["xnown_war"] = a2
                st["xnacc"][q % 2] = a2
                if q == 3:
                    xown_done[g] = a2
            done_tp(ev)

        def emit_batch_tile(j, flush=False):
            par = j % 2
            dma = O("sync", lambda e, j=j, par=par: e.dma_start(
                xt[:, par, :], xb_d[j * 128:(j + 1) * 128, :]),
                [st["x_war"][par]], out=f"x{par}", dma=True)
            xpar = j % 4

            def done_aff(aff, par=par):
                st["x_war"][par] = aff

            def done_tp(tp, xpar=xpar):
                st["xn_war"][xpar] = tp

            submit_ln(xt[:, par, :], xn_t[:, xpar, :],
                      xnT[:, 0, :, (j % 2) * 128:(j % 2) * 128 + 128],
                      [dma, st["xn_war"][xpar], st["xnT_war"], st["xnacc"][j % 2]],
                      done_aff, done_tp, flush=flush, acc=(j % 4, j // 4))

        def gen_kv_block(b2):
            col0 = b2 * 256
            kvm = []
            for k in range(DK):
                kvm.append(O("tensor", lambda e, k=k: e.matmul(
                    ps_u[:, 0:256], teff_sb[:, k, :], xnT[:, 0, k, :],
                    start=(k == 0), stop=(k == DK - 1)),
                    [st["util"], t_wA], out="mm"))
                if YIELDS and "kv" in YSET and k % 4 == 3 and k < DK - 1:
                    yield
            st["xnT_war"] = kvm[-1]
            kev = O("vector", lambda e, col0=col0: e.tensor_copy(
                kT_sb[0:64, col0:col0 + 256], ps_u[0:64, 0:256]), [kvm[-1]], out="v")
            vev = O("vector", lambda e: e.tensor_copy(
                vtmp[64:128, :], ps_u[64:128, 0:256]), [kvm[-1], st["vt_war"]], out="v")
            if YIELDS and "kv" in YSET:
                yield
            vts = []
            for tt in range(2):
                vts.append(O("tensor", lambda e, tt=tt: e.matmul(
                    ps_u[:, tt * 64:(tt + 1) * 64],
                    vtmp[64:128, tt * 128:(tt + 1) * 128],
                    ident2_sb[64:128, 0:HD], start=True, stop=True),
                    [vev, kev, t_misc], out="mm"))
            st["vt_war"] = vts[-1]
            vaev = O("vector", lambda e, b2=b2: e.tensor_copy(
                vaug_sb[:, 2 * b2:2 * b2 + 2, 0:HD],
                ps_u[:, 0:128].rearrange("p (a b) -> p a b", a=2)),
                [vts[-1]], out="v")
            sqk = O("vector", lambda e, col0=col0: e.tensor_tensor(
                sc_bf[0:64, 0:256], kT_sb[0:64, col0:col0 + 256],
                kT_sb[0:64, col0:col0 + 256], ALU.mult), [kev, st["scbf_war"]], out="v")
            k2m = O("tensor", lambda e: e.matmul(
                ps_u[0:65, 128:384], b64_sb[0:64, 0:65], sc_bf[0:64, 0:256],
                start=True, stop=True), [sqk, vaev, t_consts], out="mm")
            st["scbf_war"] = k2m
            r1 = O("vector", lambda e: e.reduce_max(stat[64:65, 21:22], ps_u[64:65, 128:384],
                                                    axis=AX.X), [k2m, st["lnmax"]], out="v")
            r2 = O("vector", lambda e: e.tensor_tensor(stat[64:65, 20:21], stat[64:65, 20:21],
                                                       stat[64:65, 21:22], ALU.max), [r1], out="v")
            st["lnmax"] = r2
            st["util"] = r1
            kv_done[b2] = p.barrier_tok("v")

        def gen_latq_q(e):
            lats = []
            for m in range(LK):
                for k in range(DK):
                    lats.append(O("tensor", lambda e_, m=m, k=k, e=e: e_.matmul(
                        ps_u[:, m * 128:(m + 1) * 128],
                        wc_sb[:, k, m * 128:(m + 1) * 128],
                        xnT_own[:, k, e * 128:(e + 1) * 128],
                        start=(k == 0), stop=(k == DK - 1)),
                        [xown_done[e], st["util"], st["latq_war"], t_wA], out="mm"))
                if YIELDS and "latqA" in YSET and len(lats) % 8 == 0:
                    yield
            lev = O("vector", lambda e_: e_.tensor_scalar_mul(
                latT[:, :, :], ps_u[:, 0:LK * 128].rearrange("p (k n) -> p k n", k=LK),
                1.0 / np.sqrt(HD)), [lats[-1]], out="v")
            st["util"] = lev
            qms = []
            for m in range(QC):
                for k in range(LK):
                    qms.append(O("tensor", lambda e_, m=m, k=k: e_.matmul(
                        ps_u[:, m * 128:(m + 1) * 128],
                        wq_sb[:, k, m * 128:(m + 1) * 128],
                        latT[:, k, :], start=(k == 0), stop=(k == LK - 1)),
                        [lev, t_wB], out="mm"))
                if YIELDS and "latqB" in YSET and len(qms) % 8 == 0:
                    yield
            qev = O("vector", lambda e_: e_.tensor_copy(
                qtmp[:, :, :], ps_u[:, :].rearrange("p (m n) -> p m n", m=QC)),
                [qms[-1]], out="v")
            st["latq_war"] = qev
            qflat = qtmp[:, :, :].rearrange("p m n -> p (m n)")
            q2toks = []
            for half in range(2):
                qsq = O("vector", lambda e_, half=half: e_.tensor_tensor(
                    sc_bf[:, :], qflat[:, half * 512:(half + 1) * 512],
                    qflat[:, half * 512:(half + 1) * 512], ALU.mult),
                    [qev, st["scbf_war"]], out="v")
                q2toks.append(O("tensor", lambda e_, half=half: e_.matmul(
                    ps_u[0:65, half * 512:(half + 1) * 512], b64_sb[:, 0:65],
                    sc_bf[:, :], start=True, stop=True), [qsq], out="mm"))
                st["scbf_war"] = q2toks[-1]
            q2a, q2b = q2toks
            m1 = O("vector", lambda e_: e_.reduce_max(q2m[64:65, 0:1], ps_u[64:65, 0:1024],
                                                      axis=AX.X), [q2a, q2b, st["q2m_war"]], out="v")
            st["util"] = m1
            qpar = e % 2
            qsv = qs[:, qpar, :, :].rearrange("p h n -> p (h n)") \
                                   .rearrange("p (m two n) -> p m two n", two=2, n=128)
            evn = O("vector", lambda e_, qsv=qsv: e_.tensor_copy(
                qsv[0:64, :, 0, :], qtmp[0:64, :, :]),
                [qev, st["qs_war2"][qpar]], out="v")
            odd = O("sync", lambda e_, qsv=qsv: e_.dma_start(
                qsv[0:64, :, 1, :], qtmp[64:128, :, :]),
                [qev, st["qs_war2"][qpar]], out=f"qo{e % 2}", dma=True)
            q_done[e] = odd
            latq_ctx[e] = (evn,)
            latq_ctx2[e] = m1

        def emit_latq_m(e):
            (evn,) = latq_ctx.pop(e)
            m2 = O("vector", lambda e_: e_.tensor_tensor(q2m[64:65, 1:2], q2m[64:65, 0:1],
                                                         stat[64:65, 20:21], ALU.mult),
                   [st["lnmax"], latq_ctx2.pop(e)], out="v")
            st["lnmax"] = m2
            m3 = newton_rsqrt_ap(m2, q2m[64:65, 1:2], q2m[64:65, 2:3], q2m_i[64:65, 0:1],
                                 q2m[64:65, 3:4], iters=1)
            m4 = O("vector", lambda e_: e_.tensor_tensor(q2m[64:65, 4:5], q2m[64:65, 1:2],
                                                         q2m[64:65, 2:3], ALU.mult),
                   [m3, st["q2m_war"]], out="v")
            qpar = e % 2
            aug = O("vector", lambda e_, e=e, qpar=qpar: e_.tensor_scalar_mul(
                qs[64:65, qpar, :, :].rearrange("p h n -> p (h n)"),
                ones_sb[64:65, 2:3].broadcast_to([1, QS_W]),
                q2m[64:65, 4:5]), [m4, evn, t_consts, st["qs_war2"][qpar]], out="v")
            st["q2m_war"] = aug
            aug_done[e] = aug

        def emit_attn_hg(e, hg, pump, ytl, flush_epi):
            U = 4 * (e + 1)
            co = hg * HGW
            qsflat = qs[:, e % 2, :, :].rearrange("p h n -> p (h n)")
            sc_last = None
            mtoks = {}
            sc_toks = {}

            def emit_av(t):
                par = t % 2
                last = None
                for nh in range(2):
                    yw = list(st["yfree"]) if (t == 0 and st["yfree"]) else []
                    last = O("tensor", lambda e_, t=t, nh=nh, par=par: e_.matmul(
                        ps_y[0:65, nh * 512:(nh + 1) * 512],
                        vaug_sb[:, t, :],
                        p_sb[:, par, nh * 512:(nh + 1) * 512],
                        start=(t == 0), stop=(t == U - 1)),
                        [mtoks[t], kv_done[t // 2]] + yw, out="mm")
                st["p_war"][par] = last
                return last

            for t in range(U):
                sb2 = t % 2
                sct = None
                for nh in range(2):
                    sct = O("tensor", lambda e_, t=t, nh=nh, sb2=sb2: e_.matmul(
                        ps_sc[:, sb2 * 1024 + nh * 512:sb2 * 1024 + (nh + 1) * 512],
                        kT_sb[0:65, t * 128:(t + 1) * 128],
                        qsflat[0:65, co + nh * 512:co + (nh + 1) * 512],
                        start=True, stop=True),
                        [st["sc_war"][sb2], kv_done[t // 2], q_done[e], aug_done[e]],
                        out="mm")
                sc_toks[t] = sct
                sc_last = sct
                if t > 0:
                    emit_av(t - 1)
                ex = O("scalar", lambda e_, sb2=sb2: e_.activation(
                    p_sb[:, sb2, :], ps_sc[:, sb2 * 1024:(sb2 + 1) * 1024], AF.Exp),
                    [sct, st["p_war"][sb2]], out="sc")
                st["sc_war"][sb2] = ex
                st["exp"] = ex
                if t >= U - 4:
                    mj = t - (U - 4)
                    pview = p_sb[:, sb2, :].rearrange("p (h n) -> p h n", n=128)
                    mtoks[t] = O("gpsimd", lambda e_, pview=pview, mj=mj: e_.tensor_tensor(
                        pview, pview,
                        masks_sb[:, mj, :].unsqueeze(1).broadcast_to([128, NHG, 128]),
                        ALU.mult), [ex, t_misc], out="g")
                else:
                    mtoks[t] = ex
                pump()
            av_last = emit_av(U - 1)
            st["qs_war2"][e % 2] = sc_last
            flush_epi()
            yev = O("vector", lambda e_: e_.tensor_copy(yf[0:64, :], ps_y[0:64, :]),
                    [av_last, st["yf_war"]], out="v")
            dcp = O("scalar", lambda e_: e_.activation(rec[64:65, :], ps_y[64:65, :],
                                                       AF.Copy), [av_last, st["rec_war"]],
                    out="sc")
            st["yfree"] = (yev, dcp)
            st["rec_war"] = dcp

            def _recip(e_):
                with nc.allow_low_precision(reason="recip feeds a bf16 matmul operand"):
                    return e_.reciprocal(recbf[64:65, :], rec[64:65, :])
            rcb = O("vector", _recip, [dcp, st["recbf_war"]], out="v")
            yf_tok = yev
            spar = e % 2

            def epilogue():
                rmm = []
                for nh in range(2):
                    rmm.append(O("tensor", lambda e_, nh=nh: e_.matmul(
                        ps_u[0:64, nh * 512:(nh + 1) * 512], onesr_sb[64:65, :],
                        recbf[64:65, nh * 512:(nh + 1) * 512],
                        start=True, stop=True), [rcb, t_consts, st["util"]], out="mm"))
                yn = O("vector", lambda e_: e_.tensor_tensor(
                    ytmp[0:64, :], yf[0:64, :], ps_u[0:64, :], ALU.mult),
                    [yf_tok, rmm[-1]] + list(st["ytmp_war"]), out="v")
                st["yf_war"] = yn
                st["recbf_war"] = rmm[-1]
                st["util"] = yn
                ytv = ytmp[:, :].rearrange("p (m two n) -> p m two n", two=2, n=128)
                pe_ = O("vector", lambda e_, spar=spar, hg=hg: e_.tensor_copy(
                    yT[0:64, spar, hg * 4:(hg + 1) * 4, :], ytv[0:64, :, 0, :]),
                    [yn, st["yT_war"][spar]], out="v")
                po_ = O("sync", lambda e_, spar=spar, hg=hg: e_.dma_start(
                    yT[64:128, spar, hg * 4:(hg + 1) * 4, :], ytv[0:64, :, 1, :]),
                    [yn, st["yT_war"][spar]], out=f"yo{spar}", dma=True)
                st["ytmp_war"] = (pe_, po_)
                ytl.append(pe_)
                ytl.append(po_)

            return epilogue

        def gen_proj(e, ytoks):
            while YIELDS and e >= 4 and (e - 4) not in combine_emitted:
                yield "stall"
            spar = e % 2
            pms = []
            for k in range(QC):
                for nh in range(2):
                    pms.append(O("tensor", lambda e_, k=k, nh=nh, spar=spar: e_.matmul(
                        ps_u[:, nh * 512:(nh + 1) * 512],
                        yT[:, spar, k, :],
                        wproj_sb[:, k, nh * 512:(nh + 1) * 512],
                        start=(k == 0), stop=(k == QC - 1)),
                        ytoks + [st["util"], t_wE], out="mm"))
                if YIELDS and "proj" in YSET and k % 2 == 1 and k < QC - 1:
                    yield
            sev = O("vector", lambda e_, e=e: e_.tensor_copy(s_f[:, e % 4, :], ps_u[:, :]),
                    [pms[-1], sf_war[e % 4]], out="v")
            st["util"] = sev
            st["yT_war"][spar] = pms[-1]
            sf_done[e % 4] = sev
            proj_emitted.add(e)
            if DBG_SLOT0 and e == 0:
                d1 = O("sync", lambda e_: e_.dma_start(dbg_yt[:, :, :], yT[:, 0, :, :]),
                       [pms[-1]] + ytoks, out="o0", dma=True)
                d2 = O("sync", lambda e_: e_.dma_start(dbg_sf[:, :], s_f[:, 0, :]),
                       [sev], out="o0", dma=True)
                d3 = O("sync", lambda e_: e_.dma_start(dbg_qs[:, :, :], qs[0:65, 0, :, :]),
                       [sev, aug_done[0], q_done[0]], out="o0", dma=True)
                st["yT_war"][spar] = d1
                sf_war[0] = d2
                st["qs_war2"][0] = d3

        wg_wb = {}
        wu_wb = {}
        wd_wb = {}
        wb_ring = {"g": [None] * 3, "u": [None] * 3, "d": [None] * 7}
        wb_chain = {}
        proj_emitted = set()
        combine_emitted = set()

        def emit_wb(tag, fn, dep):
            t = O("sync", fn, [dep, wb_chain.get(tag)], out=tag, dma=True)
            wb_chain[tag] = t
            return t

        def load_gu(m, par, first_wb, from_bf):
            gsrc = (wgbf_d if from_bf else wg_d)
            usrc = (wubf_d if from_bf else wu_d)
            eng = "sync" if from_bf else "gpsimd"
            tg = O(eng, lambda e_, m=m, par=par, gsrc=gsrc: e_.dma_start(
                wg_t[:, par, :, :],
                gsrc[:, :, m * 128:(m + 1) * 128].rearrange("k p c -> p k c")),
                [st["wg_war"][par], wb_ring["g"][par],
                 wg_wb.get(m) if from_bf else None],
                out=(f"sg{par}" if from_bf else f"fg{par}"), dma=True)
            tu = O(eng, lambda e_, m=m, par=par, usrc=usrc: e_.dma_start(
                wu_t[:, par, :, :],
                usrc[:, :, m * 128:(m + 1) * 128].rearrange("k p c -> p k c")),
                [st["wu_war"][par], wb_ring["u"][par],
                 wu_wb.get(m) if from_bf else None],
                out=(f"su{par}" if from_bf else f"fu{par}"), dma=True)
            if first_wb:
                wbg = emit_wb(f"wb{m % 2}", lambda e_, m=m, par=par: e_.dma_start(
                    wgbf_d[:, :, m * 128:(m + 1) * 128].rearrange("k p c -> p k c"),
                    wg_t[:, par, :, :]), tg)
                wg_wb[m] = wbg
                wb_ring["g"][par] = wbg
                wbu = emit_wb(f"wb{2 + m % 2}", lambda e_, m=m, par=par: e_.dma_start(
                    wubf_d[:, :, m * 128:(m + 1) * 128].rearrange("k p c -> p k c"),
                    wu_t[:, par, :, :]), tu)
                wu_wb[m] = wbu
                wb_ring["u"][par] = wbu
            return tg, tu

        def gen_ffn_half(B, half):
            tok0 = 4 * B * 128 + half * 256
            own_deps = [xown_done[4 * B + 2 * half + q] for q in range(2)]
            first_wb = (B == 0 and half == 0)
            from_bf = (B > 0)
            pend = {m0: load_gu(m0, m0, first_wb, from_bf) for m0 in range(3)}
            if YIELDS:
                yield
            h2_toks = []
            for m in range(FM):
                par = m % 3
                tg, tu = pend.pop(m)
                gm = None
                for k in range(DK):
                    gm = O("tensor", lambda e_, k=k, par=par, tok0=tok0: e_.matmul(
                        ps_u[:, 0:256], wg_t[:, par, k, :],
                        xnT_own[:, k, tok0:tok0 + 256],
                        start=(k == 0), stop=(k == DK - 1)),
                        [tg, st["util"], st["hbuf_war"]] + own_deps, out="mm")
                um = None
                for k in range(DK):
                    um = O("tensor", lambda e_, k=k, par=par, tok0=tok0: e_.matmul(
                        ps_u[:, 512:768], wu_t[:, par, k, :],
                        xnT_own[:, k, tok0:tok0 + 256],
                        start=(k == 0), stop=(k == DK - 1)), [tu], out="mm")
                tha = O("scalar", lambda e_: e_.activation(
                    t_sb[:, :], ps_u[:, 0:256], AF.Tanh, scale=cst[:, 0:1]),
                    [gm, t_consts, st["tsb_war"]], out="sc")
                h1 = O("vector", lambda e_: e_.scalar_tensor_tensor(
                    th1[:, :], t_sb[:, :], 1.0, ps_u[:, 0:256], ALU.add, ALU.mult),
                    [tha, st["th1_war"]], out="v")
                st["tsb_war"] = h1
                h2 = O("vector", lambda e_, m=m: e_.tensor_tensor(
                    h_sb[:, m, :], th1[:, :], ps_u[:, 512:768], ALU.mult),
                    [h1, um], out="v")
                st["th1_war"] = h2
                st["util"] = h2
                st["wg_war"][par] = gm
                st["wu_war"][par] = um
                h2_toks.append(h2)
                if m + 3 < FM:
                    pend[m + 3] = load_gu(m + 3, par, first_wb, from_bf)
                if YIELDS:
                    yield
            GM = 6
            accs = []
            for tc in range(2):
                eidx = 4 * B + 2 * half + tc
                accs.append((eidx, eidx % 2))
            dlast = None

            def wd_load(seq):
                nh_, m_ = divmod(seq, FM)
                sl_ = seq % 7
                src = (wdbf_d if from_bf else wd_d)
                t = O("sync" if from_bf else "gpsimd",
                      lambda e_, m_=m_, nh_=nh_, sl_=sl_, src=src:
                      e_.dma_start(
                    wd_t[:, sl_, :],
                    src[m_ * 128:(m_ + 1) * 128, nh_ * 512:(nh_ + 1) * 512]),
                    [st["wd_war"][sl_], wb_ring["d"][sl_],
                     wd_wb.get(seq) if from_bf else None],
                    out=(f"sd{sl_}" if from_bf else f"fd{sl_}"), dma=True)
                if first_wb:
                    wb = emit_wb(f"wb{seq % 4}",
                                 lambda e_, m_=m_, nh_=nh_, sl_=sl_: e_.dma_start(
                        wdbf_d[m_ * 128:(m_ + 1) * 128, nh_ * 512:(nh_ + 1) * 512],
                        wd_t[:, sl_, :]), t)
                    wd_wb[seq] = wb
                    wb_ring["d"][sl_] = wb
                return t

            wd_toks = {}
            for s0 in range(7):
                wd_toks[s0] = wd_load(s0)
            for nh in range(2):
                for g4 in range(FM // GM):
                    gtoks = []
                    for mi in range(GM):
                        m = g4 * GM + mi
                        seq = nh * FM + m
                        sl_ = seq % 7
                        tdm = wd_toks.pop(seq)
                        for tc in range(2):
                            dlast = O("tensor", lambda e_, m=m, tc=tc, sl_=sl_,
                                      mi=mi, g4=g4: e_.matmul(
                                ps_u[:, tc * 512:(tc + 1) * 512],
                                h_sb[:, m, tc * 128:(tc + 1) * 128],
                                wd_t[:, sl_, :],
                                start=(mi == 0), stop=(mi == GM - 1)),
                                [tdm, h2_toks[m], st["util"] if mi == 0 else None],
                                out="mm")
                        st["wd_war"][sl_] = dlast
                        if seq + 7 < 2 * FM:
                            wd_toks[seq + 7] = wd_load(seq + 7)
                    for tc in range(2):
                        eidx, par = accs[tc]
                        dst = o_sb[:, par, nh * 512:(nh + 1) * 512]
                        if g4 == 0:
                            ev = O("vector", lambda e_, dst=dst, tc=tc: e_.tensor_copy(
                                dst, ps_u[:, tc * 512:(tc + 1) * 512]),
                                [dlast, st["o_war"][par]], out="v")
                        else:
                            ev = O("vector", lambda e_, dst=dst, tc=tc: e_.tensor_tensor(
                                dst, dst, ps_u[:, tc * 512:(tc + 1) * 512], ALU.add),
                                [dlast], out="v")
                        gtoks.append(ev)
                    st["util"] = gtoks[-1]
                    if YIELDS:
                        yield
            for tc in range(2):
                eidx, par = accs[tc]
                sfi = 2 * half + tc
                while YIELDS and eidx not in proj_emitted:
                    yield "stall"
                facc = o_sb[:, par, :]
                xod = O("sync", lambda e_, eidx=eidx, par=par: e_.dma_start(
                    xo_t[:, par, :], xo_d[eidx * 128:(eidx + 1) * 128, :]),
                    [st["xo_war"][par]], out=f"xo{par}", dma=True)
                t2 = O("scalar", lambda e_, facc=facc: e_.activation(
                    t2_sb[:, :], facc, AF.Tanh, scale=cst[:, 1:2]),
                    [st["util"], t_consts, st["t2_war"]], out="sc")
                u1 = O("vector", lambda e_, facc=facc: e_.scalar_tensor_tensor(
                    u1_sb[:, :], t2_sb[:, :], 1.0, facc, ALU.add, ALU.mult),
                    [t2, st["u1_war"]], out="v")
                st["t2_war"] = u1
                u2 = O("vector", lambda e_, sfi=sfi: e_.scalar_tensor_tensor(
                    u2_sb[:, :], s_f[:, sfi, :], 0.25 * SIG ** 3, u1_sb[:, :],
                    ALU.mult, ALU.mult), [u1, sf_done[sfi], st["u2_war"]], out="v")
                st["u1_war"] = u2
                oc = O("vector", lambda e_, par=par: e_.tensor_tensor(
                    o_sb[:, par, :], u2_sb[:, :], xo_t[:, par, :], ALU.add),
                    [u2, xod, u1], out="v")
                od = O("sync", lambda e_, eidx=eidx, par=par: e_.dma_start(
                    out_d[eidx * 128:(eidx + 1) * 128, :], o_sb[:, par, :]),
                    [oc], out=f"o{par}", dma=True)
                st["u2_war"] = oc
                st["o_war"][par] = od
                st["xo_war"][par] = oc
                sf_war[sfi] = u2
                combine_emitted.add(eidx)
                if YIELDS:
                    yield
            st["hbuf_war"] = dlast

        from collections import deque

        vecq = deque()
        tq_must = deque()
        tq_lazy = deque()
        epiq = deque()
        act_m = [None]
        act_l = [None]
        tick = [0]
        EPI_DELAY = 12

        LAZY_STEPS = 1

        def run_all(item):
            r = item()
            if r is not None:
                for _ in r:
                    pass

        def adv(gen, steps, stash=None):
            for _ in range(steps):
                try:
                    v = next(gen)
                except StopIteration:
                    return "done"
                if v == "stall":
                    if stash is not None:
                        stash.appendleft(gen)
                    return "stall"
            return "active"

        def finish_must():
            if act_m[0] is not None:
                for _ in act_m[0]:
                    pass
                act_m[0] = None

        def lazy_step():
            if act_l[0] is not None:
                r = adv(act_l[0], LAZY_STEPS)
                if r == "stall":
                    tq_lazy.appendleft(act_l[0])
                    act_l[0] = None
                elif r == "done":
                    act_l[0] = None
                return True
            if tq_lazy:
                ent = tq_lazy.popleft()
                if callable(ent):
                    ent = ent()
                    if ent is None:
                        return True
                r = adv(ent, 1)
                if r == "active":
                    act_l[0] = ent
                elif r == "stall":
                    tq_lazy.appendleft(ent)
                return True
            return False

        def lazy_step_force():
            if not lazy_step():
                raise RuntimeError("stalled with no lazy work")

        def enqueue_group(g):
            def pair1(g=g):
                emit_batch_tile(4 * g)
                emit_batch_tile(4 * g + 1)
                tq_must.append(lambda: gen_kv_block(2 * g))
            def pair2(g=g):
                emit_batch_tile(4 * g + 2)
                emit_batch_tile(4 * g + 3)
                tq_must.append(lambda: gen_kv_block(2 * g + 1))
            def tail(g=g):
                tq_must.append(lambda: gen_latq_q(g))
                tq_must.append(lambda: (emit_latq_m(g), None)[1])
            vecq.append(pair1)
            vecq.append(pair2)
            vecq.append(tail)

        def pump():
            tick[0] += 1
            if vecq:
                vecq.popleft()()
            if act_m[0] is not None:
                if adv(act_m[0], 1) != "active":
                    act_m[0] = None
                return
            must_pending = bool(tq_must) or (bool(epiq) and
                                             tick[0] >= epiq[0][0])
            if (act_l[0] is not None or tq_lazy) and \
                    (tick[0] % 2 == 0 or not must_pending):
                lazy_step()
                return
            if tq_must:
                r = tq_must.popleft()()
                if r is not None and adv(r, 1) == "active":
                    act_m[0] = r
                return
            if epiq and tick[0] >= epiq[0][0]:
                rdy, ent = epiq.popleft()
                if callable(ent):
                    ent = ent()
                if ent is None:
                    return
                r = adv(ent, 1)
                if r == "active":
                    act_m[0] = ent
                    return
                if r != "stall":
                    return
                epiq.appendleft((rdy, ent))
                lazy_step()
                return
            lazy_step()

        def drain(q):
            finish_must()
            while q:
                run_all(q.popleft())

        enqueue_group(0)
        while vecq:
            vecq.popleft()()
        drain(tq_must)

        for e in range(S):
            if e + 1 < S:
                enqueue_group(e + 1)
            if e % 2 == 1:
                k = (e - 1) // 2
                tq_lazy.append(lambda B=k // 2, hf=k % 2: gen_ffn_half(B, hf))

            def flush_epi():
                finish_must()
                while epiq:
                    ent = epiq.popleft()[1]
                    if callable(ent):
                        ent = ent()
                    if ent is None:
                        continue
                    while True:
                        r = adv(ent, 1)
                        if r == "done":
                            break
                        if r == "stall":
                            lazy_step_force()

            ytl = []
            for hg in range(2):
                epi = emit_attn_hg(e, hg, pump, ytl, flush_epi)
                epiq.append((tick[0] + EPI_DELAY,
                             lambda epi=epi: (epi(), None)[1]))
            epiq.append((tick[0] + EPI_DELAY, lambda e=e, ytl=ytl: gen_proj(e, ytl)))
            while vecq:
                vecq.popleft()()
            drain(tq_must)

        finish_must()
        while epiq:
            ent = epiq.popleft()[1]
            if callable(ent):
                ent = ent()
            if ent is None:
                continue
            while True:
                r = adv(ent, 1)
                if r == "done":
                    break
                if r == "stall":
                    lazy_step_force()
        while tq_lazy or act_l[0] is not None:
            a = act_l[0]
            if a is not None:
                for _ in a:
                    pass
                act_l[0] = None
                continue
            ent = tq_lazy.popleft()
            if callable(ent):
                run_all(ent)
            else:
                for _ in ent:
                    pass

        O("sync", lambda e_: e_.nop(), [p.barrier_tok("o0"), p.barrier_tok("o1")])

        import os
        if os.environ.get("MK_DEBUG_SEMS"):
            print("SEMS:", {k: v[1] for k, v in p._sems.items()})
        block = es.enter_context(nc.Block())
        p.emit(block)

    return nc


def stage_core_inputs(cfg, c, x, wc, wq, wkv, wproj, wg, wu, wd):
    b, i = c // 4, c % 4
    J, S = cfg.J, cfg.S
    xb = np.ascontiguousarray(x[b])
    own_chunks = [4 * e + 3 - i for e in range(S)]
    xo = np.ascontiguousarray(
        xb.reshape(J, 128, cfg.D)[own_chunks].reshape(cfg.OWN, cfg.D))
    kt_idx = np.arange(128)[:, None]
    qt_idx = np.arange(128)[None, :]
    tri = (qt_idx >= kt_idx).astype(np.float32)
    mk = np.zeros((4, 128, 128), np.float32)
    for jj in range(4):
        if jj < 3 - i:
            mk[jj] = 1.0
        elif jj == 3 - i:
            mk[jj] = tri
    return {
        "xb": xb,
        "xo": xo,
        "wc": wc,
        "wq": wq,
        "wct": np.ascontiguousarray(wc.T),
        "wkv": wkv,
        "wproj": wproj,
        "wg": np.ascontiguousarray(wg.T.reshape(cfg.DK, 128, cfg.FH)),
        "wu": np.ascontiguousarray(wu.T.reshape(cfg.DK, 128, cfg.FH)),
        "wd": np.ascontiguousarray(wd.T),
        "ident": np.eye(128, dtype=ml_dtypes.bfloat16),
        "ident2": np.ascontiguousarray(
            np.concatenate([np.eye(cfg.HD)] * 2, axis=0).astype(ml_dtypes.bfloat16)),
        "masks": mk.astype(ml_dtypes.bfloat16),
        "sel": np.tile(np.eye(4, dtype=np.float32)[3 - i], (128, 1)),
    }


def assemble(cfg, results):
    out = np.empty((2, cfg.T, cfg.D), np.float32)
    for c in range(8):
        b, i = c // 4, c % 4
        o = np.asarray(results[c]["out"]).reshape(cfg.S, 128, cfg.D)
        for e in range(cfg.S):
            ch = 4 * e + 3 - i
            out[b, ch * 128:(ch + 1) * 128, :] = o[e]
    return out


def run(cfg, x, wc, wq, wkv, wproj, wg, wu, wd, nc=None):
    if nc is None:
        nc = build_program(cfg)
    in_maps = [stage_core_inputs(cfg, c, x, wc, wq, wkv, wproj, wg, wu, wd)
               for c in range(8)]
    res = run_bass_kernel_spmd(nc, in_maps, core_ids=list(range(8)))
    return assemble(cfg, res.results)


def kernel(L0_attn_x, L0_attn_Wc_down, L0_attn_Wq_up, L0_attn_Wkv_up,
           L0_attn_Wc_proj, L0_ffn_w_gate, L0_ffn_w_up, L0_ffn_w_down):
    cfg = FULL
    f = lambda a, d: np.ascontiguousarray(np.asarray(a, d))
    x = f(L0_attn_x, np.float32)
    return run(cfg, x, f(L0_attn_Wc_down, np.float32), f(L0_attn_Wq_up, np.float32),
               f(L0_attn_Wkv_up, np.float32), f(L0_attn_Wc_proj, np.float32),
               f(L0_ffn_w_gate, np.int32), f(L0_ffn_w_up, np.int32),
               f(L0_ffn_w_down, np.int32))
